# version 19
# baseline (speedup 1.0000x reference)
import numpy as np
import ml_dtypes
from contextlib import ExitStack

import concourse.bass as bass
import concourse.mybir as mybir
from concourse.bass import IndirectOffsetOnAxis
from concourse.bass_utils import run_bass_kernel_spmd

F32 = mybir.dt.float32
I32 = mybir.dt.int32
BF16 = mybir.dt.bfloat16
AF = mybir.ActivationFunctionType
ALU = mybir.AluOpType
AX = mybir.AxisListType

EPS_LN = 1e-5
EPS_DEN = 1e-6
NEG_BIG = -1e30
BIGS = 32768.0
TOP_K = 4

ENGS = ("v", "s", "t", "q", "g")
DQS = ("q", "g", "s")


class CFG:
    def __init__(self, n_own_chunks=16, n_pre_chunks=48, n_experts=32,
                 d_inter=768, cap=384):
        self.d = 1024
        self.ncc = 8
        self.H = 16
        self.dh = 64
        self.noc = n_own_chunks
        self.npc = n_pre_chunks
        self.E = n_experts
        self.di = d_inter
        self.icc = d_inter // 128
        self.n_own = self.noc * 128
        self.n_pre = self.npc * 128
        self.cap = cap
        self.nsub = cap // 128
        self.nslot = n_experts * cap



class Em:
    def __init__(self, nc, ctx):
        self.nc = nc
        self.eng = {"v": nc.vector, "s": nc.scalar, "t": nc.tensor,
                    "q": nc.sync, "g": nc.gpsimd}
        self.sem = {e: ctx.enter_context(nc.semaphore(f"es_{e}")) for e in ENGS}
        self.NPOOL = 8
        self.dpool = {e: [ctx.enter_context(nc.semaphore(f"ds_{e}{i}"))
                          for i in range(self.NPOOL)] for e in DQS}
        self.duse = {e: [0] * self.NPOOL for e in DQS}
        self.ndma = {e: 0 for e in DQS}
        self.ops = {e: [] for e in ENGS}
        self.ninc = {e: 0 for e in ENGS}
        self.incpts = {e: [] for e in ENGS}
        self.clock = {e: {} for e in ENGS}
        self.lastw = {}
        self.readers = {}
        self.dmaclock = {}

    def _resolve(self, tok):
        if tok[0] == "d":
            _, qs, val = tok
            return ("d%s%d" % qs, val, self.dmaclock.get(tok, {}))
        _, e, idx = tok
        for oi, val in self.incpts[e]:
            if oi >= idx:
                return ("e" + e, val, self.ops[e][oi]["clk"])
        rec = self.ops[e][idx]
        if rec.get("isdma") or rec["inst"] is None:
            inst = self.eng[e].nop()
            self.ninc[e] += 1
            inst.then_inc(self.sem[e], 1)
            nidx = len(self.ops[e])
            self.ops[e].append({"inst": inst, "clk": dict(self.clock[e])})
            self.incpts[e].append((nidx, self.ninc[e]))
            return ("e" + e, self.ninc[e], self.ops[e][nidx]["clk"])
        self.ninc[e] += 1
        rec["inst"].then_inc(self.sem[e], 1)
        self.incpts[e].append((idx, self.ninc[e]))
        return ("e" + e, self.ninc[e], rec["clk"])

    def _semhandle(self, semkey):
        if semkey[0] == "e":
            return self.sem[semkey[1:]]
        return self.dpool[semkey[1]][int(semkey[2:])]

    def _pre(self, eng, reads, writes):
        toks = []
        for k in reads:
            t = self.lastw.get(k)
            if t is not None:
                toks.append(t)
        for k in writes:
            t = self.lastw.get(k)
            if t is not None:
                toks.append(t)
            toks.extend(self.readers.get(k, ()))
        best = {}
        for tok in toks:
            if tok[0] == "e" and tok[1] == "t" and eng == "t":
                continue
            semkey, val, snap = self._resolve(tok)
            if best.get(semkey, (0, None))[0] < val:
                best[semkey] = (val, snap)
        clk = self.clock[eng]
        for semkey, (val, snap) in sorted(best.items()):
            if clk.get(semkey, 0) >= val:
                continue
            self.eng[eng].wait_ge(self._semhandle(semkey), val)
            clk[semkey] = val
            for k2, v2 in snap.items():
                if clk.get(k2, 0) < v2:
                    clk[k2] = v2

    def _post(self, eng, tok, reads, writes):
        for k in writes:
            self.lastw[k] = tok
            self.readers[k] = []
        for k in reads:
            self.readers.setdefault(k, []).append(tok)

    def op(self, eng, fn, reads=(), writes=()):
        self._pre(eng, reads, writes)
        inst = fn(self.eng[eng])
        idx = len(self.ops[eng])
        self.ops[eng].append({"inst": inst, "clk": dict(self.clock[eng])})
        self._post(eng, ("e", eng, idx), reads, writes)
        return inst

    def _dma_issue(self, queue, fn, reads, writes):
        slot = self.ndma[queue] % self.NPOOL
        self.ndma[queue] += 1
        semkey = "d%s%d" % (queue, slot)
        prev = 16 * self.duse[queue][slot]
        clk = self.clock[queue]
        if prev and clk.get(semkey, 0) < prev:
            self.eng[queue].wait_ge(self.dpool[queue][slot], prev)
            clk[semkey] = prev
        self._pre(queue, reads, writes)
        inst = fn(self.eng[queue])
        self.duse[queue][slot] += 1
        val = 16 * self.duse[queue][slot]
        inst.then_inc(self.dpool[queue][slot], 16)
        self.ops[queue].append({"inst": inst, "clk": dict(self.clock[queue]),
                                "isdma": True})
        tok = ("d", (queue, slot), val)
        self.dmaclock[tok] = dict(self.clock[queue])
        self._post(queue, tok, reads, writes)
        return inst

    def dma(self, queue, out_ap, in_ap, reads=(), writes=(), **kw):
        return self._dma_issue(
            queue, lambda e: e.dma_start(out_ap, in_ap, **kw), reads, writes)

    def idma(self, out_ap, out_off, in_ap, in_off, reads=(), writes=()):
        return self._dma_issue(
            "g", lambda e: e.indirect_dma_start(out_ap, out_off, in_ap, in_off),
            reads, writes)

    def barrier(self):
        for e in ENGS:
            if e != "q" and self.ops[e]:
                self._resolve(("e", e, len(self.ops[e]) - 1))
        q = self.eng["q"]
        clk = self.clock["q"]
        for e in ENGS:
            if e == "q":
                continue
            val = self.ninc[e]
            if val and clk.get("e" + e, 0) < val:
                q.wait_ge(self.sem[e], val)
                clk["e" + e] = val
        for qq in DQS:
            for slot in range(self.NPOOL):
                val = 16 * self.duse[qq][slot]
                semkey = "d%s%d" % (qq, slot)
                if val and clk.get(semkey, 0) < val:
                    q.wait_ge(self.dpool[qq][slot], val)
                    clk[semkey] = val
        self.ninc["q"] += 1
        qval = self.ninc["q"]
        inst = q.nop()
        inst.then_inc(self.sem["q"], 1)
        self.ops["q"].append({"inst": inst, "clk": dict(clk)})
        self.incpts["q"].append((len(self.ops["q"]) - 1, qval))
        for e in ENGS:
            if e == "q":
                continue
            self.eng[e].wait_ge(self.sem["q"], qval)
            c2 = self.clock[e]
            c2["eq"] = max(c2.get("eq", 0), qval)
            for k2, v2 in clk.items():
                if c2.get(k2, 0) < v2:
                    c2[k2] = v2

    def final_wait(self):
        clk = self.clock["q"]
        for qq in DQS:
            for slot in range(self.NPOOL):
                val = 16 * self.duse[qq][slot]
                semkey = "d%s%d" % (qq, slot)
                if val and clk.get(semkey, 0) < val:
                    self.eng["q"].wait_ge(self.dpool[qq][slot], val)
                    clk[semkey] = val



def build(c: CFG):
    nc = bass.Bass("TRN2", target_bir_lowering=False)
    D, H, dh, ncc, icc = c.d, c.H, c.dh, c.ncc, c.icc
    NOC, NPC, E, DI = c.noc, c.npc, c.E, c.di
    C, NSUB = c.cap, c.nsub

    x_own = nc.dram_tensor("x_own", [c.n_own, D], F32, kind="ExternalInput")
    x_pre = nc.dram_tensor("x_pre", [max(c.n_pre, 128), D], F32, kind="ExternalInput")
    wqkv = nc.dram_tensor("wqkv", [D, 3 * D], F32, kind="ExternalInput")
    wgate = nc.dram_tensor("wgate", [D, D], F32, kind="ExternalInput")
    wout = nc.dram_tensor("wout", [D, D], F32, kind="ExternalInput")
    gw = nc.dram_tensor("gw", [E, D], F32, kind="ExternalInput")
    Wg = nc.dram_tensor("Wg", [E, D, DI], F32, kind="ExternalInput")
    Wu = nc.dram_tensor("Wu", [E, D, DI], F32, kind="ExternalInput")
    Wd = nc.dram_tensor("Wd", [E, DI, D], F32, kind="ExternalInput")
    pmask = nc.dram_tensor("pmask", [128, NPC + NOC], BF16, kind="ExternalInput")
    ident = nc.dram_tensor("ident", [128, 128], BF16, kind="ExternalInput")
    trimask = nc.dram_tensor("trimask", [128, 512], F32, kind="ExternalInput")
    triuS = nc.dram_tensor("triuS", [128, 128], BF16, kind="ExternalInput")
    iota = nc.dram_tensor("iota", [128, 1], F32, kind="ExternalInput")
    ub = nc.dram_tensor("ub", [128, E], F32, kind="ExternalInput")
    y = nc.dram_tensor("y", [c.n_own, D], F32, kind="ExternalOutput")
    y_part = nc.dram_tensor("y_part", [c.n_own, D], F32, kind="Internal")
    xn_spill = nc.dram_tensor("xn_spill", [c.n_own, D], BF16, kind="Internal")
    wslots = nc.dram_tensor("wslots", [c.nslot + 1, 2], F32, kind="Internal")
    oslots = nc.dram_tensor("oslots", [c.nslot + 1, D], BF16, kind="Internal")

    x_own_t = x_own.ap().rearrange("(k p) ch -> k p ch", p=128)
    x_pre_t = x_pre.ap().rearrange("(k p) ch -> k p ch", p=128)
    y_part_t = y_part.ap().rearrange("(k p) ch -> k p ch", p=128)
    xn_spill_t = xn_spill.ap().rearrange("(k p) ch -> k p ch", p=128)
    y_t = y.ap().rearrange("(k p) ch -> p k ch", p=128)

    Wg_r = Wg.ap().rearrange("e (r p) f -> e p r f", p=128)
    Wu_r = Wu.ap().rearrange("e (r p) f -> e p r f", p=128)
    Wd_r = Wd.ap().rearrange("e (r p) f -> e p r f", p=128)

    ctx = ExitStack()
    em = Em(nc, ctx)
    sb = lambda name, shape, dt: ctx.enter_context(nc.sbuf_tensor(name, shape, dt))

    ident_b = sb("ident_b", [128, 128], BF16)
    maskT = sb("maskT", [128, 512], F32)
    hstat = sb("hstat", [128, 32], F32)
    pmask_b = sb("pmask_b", [128, NPC + NOC], BF16)
    gw_bf = sb("gw_bf", [E, D], BF16)
    gwT_b = sb("gwT_b", [128, ncc * E], BF16)
    combine = sb("combine", [128, NOC * E], F32)
    xc = [sb(f"xc{i}", [128, D], F32) for i in range(2)]
    stat = sb("stat", [128, 12], F32)
    eps_t = sb("eps_t", [128, 1], F32)
    lg = sb("lg", [128, max(E, 8)], F32)
    cur = sb("cur", [128, max(E, 8)], F32)
    mx8 = sb("mx8", [128, 8], F32)
    mexp = sb("mexp", [128, max(E, 8)], F32)
    ksum_sb = sb("ksum_sb", [128, ncc], F32)
    trash = sb("trash", [128, D], F32)

    triuS_b = sb("triuS_b", [128, 128], BF16)
    ones_bf = sb("ones_bf", [128, 128], BF16)
    ones_f32 = sb("ones_f32", [128, 128], F32)
    iota_b = sb("iota_b", [128, 1], F32)
    ub_sb = sb("ub_sb", [128, E], F32)
    offs_sb = sb("offs_sb", [128, E], F32)
    mask_bf = sb("mask_bf", [128, E], BF16)
    rank_sb = sb("rank_sb", [128, E], F32)
    slot_sb = sb("slot_sb", [128, E], F32)
    okm = sb("okm", [128, E], F32)
    mask2 = sb("mask2", [128, E], F32)
    neg_sb = sb("neg_sb", [128, E], F32)
    ind_sb = sb("ind_sb", [128, E], F32)
    mx8w = sb("mx8w", [128, 8], F32)
    slots4 = sb("slots4", [128, NOC * TOP_K], F32)
    pairs_sb = [sb(f"pairs{j}", [128, 2], F32) for j in range(TOP_K)]
    idx4 = [sb(f"idx4_{j}", [128, 1], I32) for j in range(TOP_K)]

    ctx_attn = ExitStack()
    psA = lambda name, shape, dt: ctx_attn.enter_context(nc.psum_tensor(name, shape, dt))
    S_ps = psA("S_ps", [128, 512], F32)
    tr_ps = psA("tr_ps", [128, 1024], BF16)
    pq = [psA(f"pq{i}", [128, 512], F32) for i in range(4)]
    scp = [psA(f"sc{i}", [128, 512], F32) for i in range(2)]

    ga = ExitStack()
    sba = lambda name, shape, dt: ga.enter_context(nc.sbuf_tensor(name, shape, dt))
    wqkv_b = sba("wqkv_b", [128, ncc * 3 * D], BF16)
    wgate_b = sba("wgate_b", [128, ncc * D], BF16)
    xnT_pre = sba("xnT_pre", [128, D], BF16)
    xnT_c = sba("xnT_c", [128, ncc * 128], BF16)
    xn_b = sba("xn_b", [128, D], BF16)
    qk_sb = sba("qk_sb", [128, 2 * D], BF16)
    qkTe = sba("qkTe", [128, 2 * D], BF16)
    Vb = sba("Vb", [128, H * (dh + 1)], BF16)
    gate_sb = sba("gate_sb", [128, D], BF16)
    tmin = sba("tmin", [128, 2 * D], F32)
    texp = sba("texp", [128, 2 * D], F32)
    S0b = sba("S0b", [128, (H // 2) * (dh + 1)], BF16)
    msc4 = [sba(f"msc4_{i}", [128, 512], BF16) for i in range(2)]
    oall = sba("oall", [128, H * (dh + 1)], F32)
    zeros_b = sba("zeros_b", [128, 512], BF16)
    z2_sb = sba("z2_sb", [128, 2 * (c.nslot // 128)], F32)
    o_sb = [sba(f"o_sb{i}", [128, D], BF16) for i in range(2)]
    oT_sb = sba("oT_sb", [128, D], BF16)
    yp_sb = [sba(f"yp_sb{i}", [128, D], F32) for i in range(2)]
    wout_b = sba("wout_b", [128, ncc * D], BF16)

    VV = Vb[:].rearrange("p (h cc) -> p h cc", cc=dh + 1)
    S0v = S0b[:].rearrange("p (h cc) -> p h cc", cc=dh + 1)
    qkTe_v = qkTe[:].rearrange("p (j t) -> p j t", t=128)
    xnT_cv = xnT_c[:].rearrange("p (j t) -> p j t", t=128)

    em.dma("q", ident_b[:], ident.ap(), writes=["ident_b"])
    em.dma("q", maskT[:], trimask.ap(), writes=["maskT"])
    em.dma("q", pmask_b[:], pmask.ap(), writes=["pmask_b"])
    em.dma("q", triuS_b[:], triuS.ap(), writes=["triuS_b"])
    em.dma("q", iota_b[:], iota.ap(), writes=["iota_b"])
    em.dma("q", ub_sb[:], ub.ap(), writes=["ub_sb"])
    em.dma("g", gw_bf[:], gw.ap(), writes=["gw_bf"])
    em.dma("g", wqkv_b[:].rearrange("p (r f) -> p r f", f=3 * D),
           wqkv.ap().rearrange("(r p) f -> p r f", p=128), writes=["wqkv_b"])
    em.dma("g", wgate_b[:].rearrange("p (r f) -> p r f", f=D),
           wgate.ap().rearrange("(r p) f -> p r f", p=128), writes=["wgate_b"])
    em.dma("g", wout_b[:].rearrange("p (r f) -> p r f", f=D),
           wout.ap().rearrange("(r p) f -> p r f", p=128), writes=["wout_b"])
    em.op("v", lambda v: v.memset(VV[:, :, dh:dh + 1], 1.0), writes=["Vb"])
    em.op("v", lambda v: v.memset(ksum_sb[:], 0.0), writes=["ksum_sb"])
    em.op("v", lambda v: v.memset(eps_t[:], EPS_LN), writes=["eps_t"])
    em.op("v", lambda v: v.memset(ones_bf[:], 1.0), writes=["ones_bf"])
    em.op("v", lambda v: v.memset(ones_f32[:], 1.0), writes=["ones_f32"])
    em.op("v", lambda v: v.memset(z2_sb[:], 0.0), writes=["z2_sb"])
    em.op("v", lambda v: v.tensor_scalar_add(offs_sb[0:1, :], ub_sb[0:1, :],
                                             float(-C)),
          reads=["ub_sb"], writes=["offs"])
    em.dma("g", wslots.ap()[0:c.nslot, :].rearrange("(p s) cc -> p s cc", p=128),
           z2_sb[:].rearrange("p (s cc) -> p s cc", cc=2),
           reads=["z2_sb"], writes=["wsl_init"])
    em.dma("g", wslots.ap()[c.nslot:c.nslot + 1, :], z2_sb[0:1, 0:2],
           reads=["z2_sb"], writes=["wsl_init2"])

    for j in range(ncc):
        em.op("t", lambda t, j=j: t.transpose(
            tr_ps[:, j * E:(j + 1) * E], gw_bf[0:E, j * 128:(j + 1) * 128],
            ident_b[0:E, 0:E]),
            reads=["gw_bf", "ident_b"], writes=["tr_ps"])
    em.op("v", lambda v: v.tensor_copy(gwT_b[:], tr_ps[:, 0:ncc * E]),
          reads=["tr_ps"], writes=["gwT_b"])
    em.op("v", lambda v: v.memset(zeros_b[:], 0.0), writes=["zeros_b"])
    em.op("t", lambda t: t.matmul(S_ps[:], zeros_b[:, 0:128], zeros_b[:],
                                  start=True, stop=False,
                                  skip_group_check=True),
          reads=["zeros_b"], writes=["S_ps"])

    def layernorm(xbuf, xkey):
        em.op("v", lambda v: v.reduce_sum(stat[:, 0:1], xbuf[:], axis=AX.X),
              reads=[xkey], writes=["st_s"])
        em.op("v", lambda v: v.tensor_scalar_mul(stat[:, 0:1], stat[:, 0:1], 1.0 / D),
              reads=["st_s"], writes=["st_m"])
        em.op("s", lambda s: s.activation(trash[:], xbuf[:], AF.Square,
                                          accum_out=stat[:, 1:2]),
              reads=[xkey], writes=["trash", "st_ssq"])
        em.op("v", lambda v: v.tensor_scalar(stat[:, 2:3], stat[:, 0:1],
                                             stat[:, 0:1], -1.0,
                                             op0=ALU.mult, op1=ALU.mult),
              reads=["st_m"], writes=["st_nb"])
        em.op("v", lambda v: v.tensor_add(stat[:, 2:3], stat[:, 2:3], eps_t[:]),
              reads=["st_nb", "eps_t"], writes=["st_nb"])
        em.op("s", lambda s: s.activation(stat[:, 3:4], stat[:, 1:2], AF.Ln,
                                          bias=stat[:, 2:3], scale=1.0 / D),
              reads=["st_ssq", "st_nb"], writes=["st_tln"])
        em.op("s", lambda s: s.activation(stat[:, 4:5], stat[:, 3:4], AF.Exp,
                                          scale=-0.5),
              reads=["st_tln"], writes=["st_rstd"])
        em.op("v", lambda v: v.tensor_scalar(xn_b[:], xbuf[:], stat[:, 0:1],
                                             stat[:, 4:5],
                                             op0=ALU.subtract, op1=ALU.mult),
              reads=[xkey, "st_m", "st_rstd"], writes=["xn_b"])

    def transpose8(src_ap, src_key, dst_ap, dst_key, bank, bank_key, ident_ap,
                   eng="v"):
        for j in range(ncc):
            em.op("t", lambda t, j=j: t.transpose(
                bank[:, j * 128:(j + 1) * 128],
                src_ap[:, j * 128:(j + 1) * 128], ident_ap),
                reads=[src_key, "ident_b"], writes=[bank_key])
        bank3 = bank[:].rearrange("p (j t) -> p j t", t=128)
        if eng == "v":
            em.op("v", lambda v: v.tensor_copy(dst_ap, bank3[:, 0:ncc, :]),
                  reads=[bank_key], writes=[dst_key])
        else:
            em.op("s", lambda s: s.copy(dst_ap, bank3[:, 0:ncc, :]),
                  reads=[bank_key], writes=[dst_key])

    def elu1(psrc_ap, pkey, toff, dst_ap, dst_key):
        em.op("v", lambda v: v.tensor_scalar_min(tmin[:, toff:toff + 512],
                                                 psrc_ap, 0.0),
              reads=[pkey], writes=["tmin"])
        em.op("s", lambda s: s.activation(texp[:, toff:toff + 512],
                                          tmin[:, toff:toff + 512], AF.Exp),
              reads=["tmin"], writes=["texp"])
        em.op("v", lambda v: v.scalar_tensor_tensor(
            dst_ap, psrc_ap, 0.0, texp[:, toff:toff + 512],
            op0=ALU.max, op1=ALU.add),
            reads=[pkey, "texp"], writes=[dst_key])

    def kv_state_update(chunk_col):
        for b in range(ncc):
            em.op("t", lambda t, b=b: t.matmul(
                scp[1][:, b:b + 1], qk_sb[:, D + b * 128:D + (b + 1) * 128],
                pmask_b[:, chunk_col:chunk_col + 1], start=True, stop=True),
                reads=["qk_sb", "pmask_b"], writes=["sc1"])
        em.op("v", lambda v: v.tensor_add(ksum_sb[:], ksum_sb[:], scp[1][:, 0:ncc]),
              reads=["sc1", "ksum_sb"], writes=["ksum_sb"])
        for h in range(H):
            em.op("t", lambda t, h=h: t.matmul(
                S_ps[64 * (h % 2):64 * (h % 2) + 64,
                     dh * (h // 2):dh * (h // 2) + dh],
                qk_sb[:, D + dh * h:D + dh * (h + 1)],
                VV[:, h, 0:dh],
                start=False, stop=False, skip_group_check=True),
                reads=["qk_sb", "Vb"], writes=["S_ps"])

    if NPC > 0:
        em.dma("q", xc[0][:], x_pre_t[0, :, :], writes=["xc0"])
    elif NOC > 0:
        em.dma("q", xc[0][:], x_own_t[0, :, :], writes=["xc0"])
    if NPC > 0:
        layernorm(xc[0], "xc0")
        transpose8(xn_b[:], "xn_b",
                   xnT_pre[:].rearrange("p (j t) -> p j t", t=128), "xnT_pre",
                   tr_ps, "tr_ps", ident_b[:], eng="s")
    for k in range(NPC):
        if k + 1 < NPC:
            em.dma("q", xc[(k + 1) % 2][:], x_pre_t[k + 1, :, :],
                   writes=[f"xc{(k + 1) % 2}"])
        elif NOC > 0:
            em.dma("q", xc[(k + 1) % 2][:], x_own_t[0, :, :],
                   writes=[f"xc{(k + 1) % 2}"])
        for b in range(4):
            for j in range(ncc):
                em.op("t", lambda t, j=j, b=b: t.matmul(
                    pq[b][:], xnT_pre[:, j * 128:(j + 1) * 128],
                    wqkv_b[:, j * 3 * D + D + b * 512:j * 3 * D + D + (b + 1) * 512],
                    start=(j == 0), stop=(j == ncc - 1)),
                    reads=["xnT_pre", "wqkv_b"], writes=[f"pq{b}"])
        elu1(pq[0][:], "pq0", 0, qk_sb[:, D:D + 512], "qk_sb")
        elu1(pq[1][:], "pq1", 512, qk_sb[:, D + 512:2 * D], "qk_sb")
        em.op("s", lambda s: s.copy(
            VV[:, 0:8, 0:dh], pq[2][:].rearrange("p (h cc) -> p h cc", cc=dh)),
            reads=["pq2"], writes=["Vb"])
        em.op("s", lambda s: s.copy(
            VV[:, 8:16, 0:dh], pq[3][:].rearrange("p (h cc) -> p h cc", cc=dh)),
            reads=["pq3"], writes=["Vb"])
        if k + 1 < NPC:
            layernorm(xc[(k + 1) % 2], f"xc{(k + 1) % 2}")
            transpose8(xn_b[:], "xn_b",
                       xnT_pre[:].rearrange("p (j t) -> p j t", t=128), "xnT_pre",
                       tr_ps, "tr_ps", ident_b[:], eng="s")
        kv_state_update(k)

    wsl_keys = ["wsl_init"]
    xnsp_keys = []

    def own_ln(k):
        par = (NPC + k) % 2
        layernorm(xc[par], f"xc{par}")
        em.dma("q", xn_spill_t[k, :, :], xn_b[:], reads=["xn_b"],
               writes=[f"xnsp_{k}"])
        xnsp_keys.append(f"xnsp_{k}")
        transpose8(xn_b[:], "xn_b", xnT_cv, "xnT_c", tr_ps, "tr_ps", ident_b[:],
                   eng="s")

    if NOC > 0:
        own_ln(0)
    for k in range(NOC):
        par = (NPC + k) % 2
        if k + 1 < NOC:
            em.dma("q", xc[(par + 1) % 2][:], x_own_t[k + 1, :, :],
                   writes=[f"xc{(par + 1) % 2}"])

        def xl(j):
            return xnT_c[:, j * 128:(j + 1) * 128]

        for j in range(ncc):
            em.op("t", lambda t, j=j: t.matmul(
                scp[0][:, 0:E], xl(j), gwT_b[:, j * E:(j + 1) * E],
                start=(j == 0), stop=(j == ncc - 1)),
                reads=["xnT_c", "gwT_b"], writes=["sc0"])
        em.op("v", lambda v: v.tensor_copy(lg[:, 0:E], scp[0][:, 0:E]),
              reads=["sc0"], writes=["lg"])
        em.op("v", lambda v: v.max(out=mx8[:], in_=lg[:, 0:E]),
              reads=["lg"], writes=["mx8"])
        em.op("v", lambda v: v.memset(mx8[:, TOP_K:8], NEG_BIG), writes=["mx8"])
        em.op("v", lambda v: v.match_replace(cur[:, 0:E], mx8[:], lg[:, 0:E],
                                             NEG_BIG),
              reads=["mx8", "lg"], writes=["cur"])
        em.op("v", lambda v: v.tensor_single_scalar(cur[:, 0:E], cur[:, 0:E],
                                                    -1e20, ALU.is_lt),
              reads=["cur"], writes=["cur"])
        em.op("v", lambda v: v.reduce_max(stat[:, 5:6], lg[:, 0:E], axis=AX.X,
                                          negate=True),
              reads=["lg"], writes=["st_negm"])
        em.op("s", lambda s: s.activation(mexp[:, 0:E], lg[:, 0:E], AF.Exp,
                                          bias=stat[:, 5:6]),
              reads=["lg", "st_negm"], writes=["mexp"])
        em.op("v", lambda v: v.tensor_mul(mexp[:, 0:E], mexp[:, 0:E], cur[:, 0:E]),
              reads=["mexp", "cur"], writes=["mexp"])
        em.op("v", lambda v: v.reduce_sum(stat[:, 6:7], mexp[:, 0:E], axis=AX.X),
              reads=["mexp"], writes=["st_den"])
        em.op("v", lambda v: v.reciprocal(stat[:, 7:8], stat[:, 6:7]),
              reads=["st_den"], writes=["st_rden"])
        em.op("v", lambda v, k=k: v.tensor_scalar_mul(
            combine[:, k * E:(k + 1) * E], mexp[:, 0:E], stat[:, 7:8]),
            reads=["mexp", "st_rden"], writes=[f"combine_{k}"])

        for b in range(4):
            for j in range(ncc):
                em.op("t", lambda t, j=j, b=b: t.matmul(
                    pq[b][:], xl(j),
                    wqkv_b[:, j * 3 * D + b * 512:j * 3 * D + (b + 1) * 512],
                    start=(j == 0), stop=(j == ncc - 1)),
                    reads=["xnT_c", "wqkv_b"], writes=[f"pq{b}"])
        for b in range(4):
            elu1(pq[b][:], f"pq{b}", b * 512,
                 qk_sb[:, b * 512:(b + 1) * 512], "qk_sb")
        transpose8(qk_sb[:, 0:D], "qk_sb", qkTe_v[:, 0:8, :], "qkTe",
                   tr_ps, "tr_ps", ident_b[:], eng="s")
        transpose8(qk_sb[:, D:2 * D], "qk_sb", qkTe_v[:, 8:16, :], "qkTe",
                   tr_ps, "tr_ps", ident_b[:], eng="s")

        for b in range(4):
            for j in range(ncc):
                if b < 2:
                    wsrc, wkey = wqkv_b, "wqkv_b"
                    wcol = j * 3 * D + 2 * D + b * 512
                else:
                    wsrc, wkey = wgate_b, "wgate_b"
                    wcol = j * D + (b - 2) * 512
                em.op("t", lambda t, j=j, b=b, wsrc=wsrc, wcol=wcol: t.matmul(
                    pq[b][:], xl(j), wsrc[:, wcol:wcol + 512],
                    start=(j == 0), stop=(j == ncc - 1)),
                    reads=["xnT_c", wkey], writes=[f"pq{b}"])
        em.op("s", lambda s: s.copy(
            VV[:, 0:8, 0:dh], pq[0][:].rearrange("p (h cc) -> p h cc", cc=dh)),
            reads=["pq0"], writes=["Vb"])
        em.op("s", lambda s: s.copy(
            VV[:, 8:16, 0:dh], pq[1][:].rearrange("p (h cc) -> p h cc", cc=dh)),
            reads=["pq1"], writes=["Vb"])
        em.op("s", lambda s: s.activation(gate_sb[:, 0:512], pq[2][:], AF.Sigmoid),
              reads=["pq2"], writes=["gate_sb"])
        em.op("s", lambda s: s.activation(gate_sb[:, 512:D], pq[3][:], AF.Sigmoid),
              reads=["pq3"], writes=["gate_sb"])

        if k + 1 < NOC:
            own_ln(k + 1)
        em.op("s", lambda s: s.copy(
            S0v[:, :, 0:dh], S_ps[:].rearrange("p (h cc) -> p h cc", cc=dh)),
            reads=["S_ps"], writes=["S0b"])
        em.op("s", lambda s: s.copy(
            S0v[:, :, dh:dh + 1],
            ksum_sb[:].rearrange("p (j o) -> p j o", o=1)),
            reads=["ksum_sb"], writes=["S0b"])

        kv_state_update(NPC + k)
        o_cur, o_key = o_sb[k % 2], f"o_sb{k % 2}"

        def score(h):
            po = 64 * (h % 2)
            ft_q, ft_k = h // 2, 8 + h // 2
            sbank, skey = scp[h % 2], f"sc{h % 2}"
            mt, mi = (h % 8) // 4, h % 4
            em.op("t", lambda t, po=po, ft_k=ft_k, ft_q=ft_q, sbank=sbank: t.matmul(
                sbank[:, 0:128],
                qkTe[po:po + 64, ft_k * 128:ft_k * 128 + 128],
                qkTe[po:po + 64, ft_q * 128:ft_q * 128 + 128],
                start=True, stop=True),
                reads=["qkTe"], writes=[skey])
            em.op("v", lambda v, mt=mt, mi=mi, sbank=sbank: v.tensor_mul(
                msc4[mt][:, mi * 128:(mi + 1) * 128], sbank[:, 0:128],
                maskT[:, 0:128]),
                reads=[skey, "maskT"], writes=[f"msc4_{mt}_{mi}"])

        score(0)
        score(1)
        for h in range(H):
            po = 64 * (h % 2)
            ft_q = h // 2
            obank, okey = pq[h % 4], f"pq{h % 4}"
            mt, mi = (h % 8) // 4, h % 4
            em.op("t", lambda t, h=h, mt=mt, mi=mi, obank=obank: t.matmul(
                obank[:, 0:dh + 1], msc4[mt][:, mi * 128:(mi + 1) * 128],
                Vb[:, h * (dh + 1):(h + 1) * (dh + 1)],
                start=True, stop=False),
                reads=[f"msc4_{mt}_{mi}", "Vb"], writes=[okey])
            em.op("t", lambda t, h=h, po=po, ft_q=ft_q, obank=obank: t.matmul(
                obank[:, 0:dh + 1],
                qkTe[po:po + 64, ft_q * 128:ft_q * 128 + 128],
                S0b[po:po + 64, (dh + 1) * (h // 2):(dh + 1) * (h // 2 + 1)],
                start=False, stop=True),
                reads=["qkTe", "S0b"], writes=[okey])
            if h + 2 < H:
                score(h + 2)
            em.op("s", lambda s, h=h, obank=obank: s.copy(
                oall[:, h * (dh + 1):(h + 1) * (dh + 1)], obank[:, 0:dh + 1]),
                reads=[okey], writes=[f"oall{h // 8}"])
            if h % 8 == 7:
                hf = h // 8
                oallv = oall[:].rearrange("p (hh cc) -> p hh cc", cc=dh + 1)
                em.op("v", lambda v, hf=hf, oallv=oallv: v.tensor_scalar_add(
                    hstat[:, hf * 8:hf * 8 + 8].rearrange(
                        "p (hh o) -> p hh o", o=1),
                    oallv[:, hf * 8:hf * 8 + 8, dh:dh + 1], EPS_DEN),
                    reads=[f"oall{hf}"], writes=[f"hdn{hf}"])
                em.op("v", lambda v, hf=hf: v.reciprocal(
                    hstat[:, 16 + hf * 8:24 + hf * 8],
                    hstat[:, hf * 8:hf * 8 + 8]),
                    reads=[f"hdn{hf}"], writes=[f"hrec{hf}"])
                for h2 in range(hf * 8, hf * 8 + 8):
                    em.op("v", lambda v, h2=h2, hf=hf: v.scalar_tensor_tensor(
                        o_cur[:, dh * h2:dh * (h2 + 1)],
                        oall[:, h2 * (dh + 1):h2 * (dh + 1) + dh],
                        hstat[:, 16 + h2:17 + h2],
                        gate_sb[:, dh * h2:dh * (h2 + 1)],
                        op0=ALU.mult, op1=ALU.mult),
                        reads=[f"oall{hf}", f"hrec{hf}", "gate_sb"],
                        writes=[o_key])
        transpose8(o_cur[:], o_key, oT_sb[:].rearrange("p (j t) -> p j t", t=128),
                   "oT_sb", tr_ps, "tr_ps", ident_b[:], eng="s")
        for half in range(2):
            for j in range(ncc):
                em.op("t", lambda t, j=j, half=half: t.matmul(
                    pq[2 + half][:, 0:512],
                    oT_sb[:, j * 128:(j + 1) * 128],
                    wout_b[:, j * D + half * 512:j * D + (half + 1) * 512],
                    start=(j == 0), stop=(j == ncc - 1)),
                    reads=["oT_sb", "wout_b"], writes=[f"pq{2 + half}"])
        for half in range(2):
            em.op("v", lambda v, half=half, par=par, k=k: v.tensor_add(
                yp_sb[k % 2][:, half * 512:(half + 1) * 512],
                pq[2 + half][:, 0:512],
                xc[par][:, half * 512:(half + 1) * 512]),
                reads=[f"pq{2 + half}", f"xc{par}"], writes=[f"yp{k % 2}"])
        em.dma("q", y_part_t[k, :, :], yp_sb[k % 2][:], reads=[f"yp{k % 2}"],
               writes=[f"ypart_{k}"])
        em.op("v", lambda v: v.tensor_copy(mask_bf[:], cur[:, 0:E]),
              reads=["cur"], writes=["mask_bf"])
        em.op("t", lambda t: t.matmul(scp[1][:, 128:160], triuS_b[:], mask_bf[:],
                                      start=True, stop=True),
              reads=["triuS_b", "mask_bf"], writes=["sc1"])
        em.op("t", lambda t: t.matmul(scp[1][:, 160:192], ones_f32[0:1, :],
                                      offs_sb[0:1, :], start=True, stop=True),
              reads=["ones_f32", "offs"], writes=["sc1"])
        em.op("t", lambda t: t.matmul(scp[1][0:1, 192:224], ones_bf[:, 0:1],
                                      mask_bf[:], start=True, stop=True),
              reads=["ones_bf", "mask_bf"], writes=["sc1"])
        em.op("v", lambda v: v.tensor_copy(rank_sb[:], scp[1][:, 128:160]),
              reads=["sc1"], writes=["rank_sb"])
        em.op("v", lambda v: v.tensor_add(slot_sb[:], rank_sb[:],
                                          scp[1][:, 160:192]),
              reads=["rank_sb", "sc1"], writes=["slot_sb"])
        em.op("v", lambda v: v.tensor_add(offs_sb[0:1, :], offs_sb[0:1, :],
                                          scp[1][0:1, 192:224]),
              reads=["offs", "sc1"], writes=["offs"])
        em.op("v", lambda v: v.tensor_tensor(okm[:], slot_sb[:], ub_sb[:],
                                             op=ALU.is_lt),
              reads=["slot_sb", "ub_sb"], writes=["okm"])
        em.op("v", lambda v: v.tensor_mul(mask2[:], okm[:], cur[:, 0:E]),
              reads=["okm", "cur"], writes=["mask2"])
        em.op("v", lambda v: v.tensor_scalar_add(slot_sb[:], slot_sb[:], -BIGS),
              reads=["slot_sb"], writes=["slot_sb"])
        em.op("v", lambda v: v.tensor_mul(slot_sb[:], slot_sb[:], mask2[:]),
              reads=["slot_sb", "mask2"], writes=["slot_sb"])
        em.op("v", lambda v: v.tensor_scalar_add(slot_sb[:], slot_sb[:], BIGS),
              reads=["slot_sb"], writes=["slot_sb"])
        em.op("v", lambda v: v.tensor_scalar(neg_sb[:], slot_sb[:], -1.0, BIGS,
                                             op0=ALU.mult, op1=ALU.add),
              reads=["slot_sb"], writes=["neg_sb"])
        em.op("v", lambda v: v.max(out=mx8w[:], in_=neg_sb[:]),
              reads=["neg_sb"], writes=["mx8w"])
        em.op("v", lambda v, k=k: v.tensor_scalar(
            slots4[:, 4 * k:4 * k + 4], mx8w[:, 0:4], -1.0, BIGS,
            op0=ALU.mult, op1=ALU.add),
            reads=["mx8w"], writes=["slots4"])
        em.op("v", lambda v, k=k: v.tensor_scalar_min(
            slots4[:, 4 * k:4 * k + 4], slots4[:, 4 * k:4 * k + 4],
            float(c.nslot)),
            reads=["slots4"], writes=["slots4"])
        for j in range(TOP_K):
            em.op("v", lambda v, j=j: v.tensor_single_scalar(
                ind_sb[:], neg_sb[:], mx8w[:, j:j + 1], ALU.is_equal),
                reads=["neg_sb", "mx8w"], writes=["ind_sb"])
            em.op("v", lambda v, j=j, k=k: v.tensor_mul(
                ind_sb[:], ind_sb[:], combine[:, k * E:(k + 1) * E]),
                reads=["ind_sb", f"combine_{k}"], writes=["ind_sb"])
            em.op("v", lambda v, j=j: v.reduce_sum(
                pairs_sb[j][:, 1:2], ind_sb[:], axis=AX.X),
                reads=["ind_sb"], writes=[f"pairs{j}"])
            em.op("v", lambda v, j=j, k=k: v.tensor_scalar_add(
                pairs_sb[j][:, 0:1], iota_b[:], float(k * 128)),
                reads=["iota_b"], writes=[f"pairs{j}"])
            em.op("v", lambda v, j=j, k=k: v.tensor_copy(
                idx4[j][:], slots4[:, 4 * k + j:4 * k + j + 1]),
                reads=["slots4"], writes=[f"idx4_{j}"])
            em.idma(wslots.ap(), IndirectOffsetOnAxis(idx4[j][:], axis=0),
                    pairs_sb[j][:], None,
                    reads=[f"idx4_{j}", f"pairs{j}", "wsl_init", "wsl_init2"],
                    writes=[f"wsl_{k}_{j}"])
            wsl_keys.append(f"wsl_{k}_{j}")


    em.barrier()
    ga.close()
    ctx_attn.close()

    gc = ExitStack()
    sbc = lambda name, shape, dt: gc.enter_context(nc.sbuf_tensor(name, shape, dt))
    NR = 3
    Wg_b = [sbc(f"Wg_b{i}", [128, ncc * DI], BF16) for i in range(NR)]
    Wu_b = [sbc(f"Wu_b{i}", [128, ncc * DI], BF16) for i in range(NR)]
    Wd_b = [sbc(f"Wd_b{i}", [128, icc * D], BF16) for i in range(NR)]
    pe_sb = [sbc(f"pe{i}", [128, 2 * NSUB], F32) for i in range(NR)]
    ie_sb = [[sbc(f"ie{i}_{s}", [128, 1], I32) for s in range(NSUB)]
             for i in range(NR)]
    xg_sb = [[sbc(f"xg{i}_{s}", [128, D], BF16) for s in range(NSUB)]
             for i in range(2)]
    xgT = [sbc(f"xgT{i}", [128, ncc * C], BF16) for i in range(2)]
    aT = sbc("aT", [128, icc * C], BF16)
    sg = [sbc(f"sg{i}", [128, C], F32) for i in range(2)]
    os_sb = [sbc(f"os_sb{i}", [128, D], BF16) for i in range(2)]
    og_sb = [sbc(f"og{i}", [128, D], BF16) for i in range(2)]
    ctx_c = ExitStack()
    gq = [ctx_c.enter_context(nc.psum_tensor(f"gq{i}", [128, C], F32))
          for i in range(2)]
    uq = [ctx_c.enter_context(nc.psum_tensor(f"uq{i}", [128, C], F32))
          for i in range(2)]
    wd_ps = ctx_c.enter_context(nc.psum_tensor("wdp", [128, 1024], F32))
    trF = ctx_c.enter_context(nc.psum_tensor("trF", [128, 1024], BF16))

    xgT_v = [xgT[i][:].rearrange("p (j t) -> p j t", t=C) for i in range(2)]
    osl_keys = ["osl_z"]

    em.op("v", lambda v: v.memset(os_sb[0][:], 0.0), writes=["os0"])
    em.dma("s", oslots.ap()[c.nslot:c.nslot + 1, :], os_sb[0][0:1, :],
           reads=["os0"], writes=["osl_z"])

    def load_expert(e):
        par = e % NR
        em.dma("g", Wg_b[par][:].rearrange("p (r f) -> p r f", f=DI),
               Wg_r[e, :, :, :], writes=[f"Wg{par}"])
        em.dma("g", Wu_b[par][:].rearrange("p (r f) -> p r f", f=DI),
               Wu_r[e, :, :, :], writes=[f"Wu{par}"])
        em.dma("g", Wd_b[par][:].rearrange("p (r f) -> p r f", f=D),
               Wd_r[e, :, :, :], writes=[f"Wd{par}"])
        em.dma("q", pe_sb[par][:].rearrange("p (s cc) -> p s cc", cc=2),
               wslots.ap()[e * C:(e + 1) * C, :].rearrange(
                   "(s p) cc -> p s cc", p=128),
               reads=wsl_keys, writes=[f"pe{par}"])

    def gather_stage(e):
        par = e % NR
        xpar = e % 2
        pv = pe_sb[par][:].rearrange("p (s cc) -> p s cc", cc=2)
        for s in range(NSUB):
            em.op("v", lambda v, s=s, par=par: v.tensor_copy(
                ie_sb[par][s][:], pv[:, s, 0:1]),
                reads=[f"pe{par}"], writes=[f"ie{par}{s}"])
            em.idma(xg_sb[xpar][s][:], None, xn_spill.ap(),
                    IndirectOffsetOnAxis(ie_sb[par][s][:], axis=0),
                    reads=[f"ie{par}{s}"] + xnsp_keys, writes=[f"xg{xpar}{s}"])

    def trans_stage(e):
        par = e % 2
        for s in range(NSUB):
            for j in range(ncc):
                em.op("t", lambda t, j=j, s=s, par=par: t.transpose(
                    trF[:, j * 128:(j + 1) * 128],
                    xg_sb[par][s][:, j * 128:(j + 1) * 128], ident_b[:]),
                    reads=[f"xg{par}{s}", "ident_b"], writes=["trF"])
            em.op("s", lambda s_, s=s, par=par: s_.copy(
                xgT_v[par][:, :, s * 128:(s + 1) * 128],
                trF[:].rearrange("p (j t) -> p j t", t=128)),
                reads=["trF"], writes=[f"xgT{par}"])

    def gu_stage(e):
        par = e % NR
        xpar = e % 2
        for ic in range(icc):
            w = ic % 2
            for j in range(ncc):
                em.op("t", lambda t, j=j, ic=ic, w=w, par=par, xpar=xpar: t.matmul(
                    gq[w][:, 0:C],
                    Wg_b[par][:, j * DI + ic * 128:j * DI + (ic + 1) * 128],
                    xgT[xpar][:, j * C:(j + 1) * C],
                    start=(j == 0), stop=(j == ncc - 1)),
                    reads=[f"Wg{par}", f"xgT{xpar}"], writes=[f"gq{w}"])
            for j in range(ncc):
                em.op("t", lambda t, j=j, ic=ic, w=w, par=par, xpar=xpar: t.matmul(
                    uq[w][:, 0:C],
                    Wu_b[par][:, j * DI + ic * 128:j * DI + (ic + 1) * 128],
                    xgT[xpar][:, j * C:(j + 1) * C],
                    start=(j == 0), stop=(j == ncc - 1)),
                    reads=[f"Wu{par}", f"xgT{xpar}"], writes=[f"uq{w}"])
            em.op("s", lambda s, w=w: s.activation(sg[w][:], gq[w][:, 0:C],
                                                   AF.Silu),
                  reads=[f"gq{w}"], writes=[f"sg{w}"])
            em.op("v", lambda v, ic=ic, w=w: v.tensor_mul(
                aT[:, ic * C:(ic + 1) * C], sg[w][:], uq[w][:, 0:C]),
                reads=[f"sg{w}", f"uq{w}"], writes=[f"aT_{ic}"])

    def wd_stage(e):
        par = e % NR
        pv = pe_sb[par][:].rearrange("p (s cc) -> p s cc", cc=2)
        for s in range(NSUB):
            for ic in range(icc):
                for half in range(2):
                    em.op("t", lambda t, s=s, ic=ic, half=half, par=par: t.matmul(
                        wd_ps[:, half * 512:(half + 1) * 512],
                        aT[:, ic * C + s * 128:ic * C + s * 128 + 128],
                        Wd_b[par][:, ic * D + half * 512:ic * D + (half + 1) * 512],
                        start=(ic == 0), stop=(ic == icc - 1)),
                        reads=[f"aT_{ic}", f"Wd{par}"], writes=["wdp"])
            w = s % 2
            em.op("v", lambda v, s=s, w=w, par=par: v.tensor_scalar_mul(
                os_sb[w][:], wd_ps[:], pv[:, s, 1:2]),
                reads=["wdp", f"pe{par}"], writes=[f"os{w}"])
            em.dma("s", oslots.ap()[e * C + s * 128:e * C + (s + 1) * 128, :],
                   os_sb[w][:], reads=[f"os{w}"], writes=[f"osl_{e}_{s}"])
            osl_keys.append(f"osl_{e}_{s}")

    load_expert(0)
    gather_stage(0)
    load_expert(1)
    gather_stage(1)
    load_expert(2)
    for e in range(E):
        trans_stage(e)
        gu_stage(e)
        wd_stage(e)
        if e + 3 < E:
            load_expert(e + 3)
        if e + 2 < E:
            gather_stage(e + 2)

    ya_sb = [gc.enter_context(nc.sbuf_tensor(f"ya{i}", [128, D], F32))
             for i in range(2)]
    for k in range(NOC):
        em.dma("q", ya_sb[k % 2][:], y_part_t[k, :, :],
               reads=[f"ypart_{k}"], writes=[f"ya{k % 2}"])
        for j in range(TOP_K):
            em.op("v", lambda v, j=j, k=k: v.tensor_copy(
                idx4[j][:], slots4[:, 4 * k + j:4 * k + j + 1]),
                reads=["slots4"], writes=[f"idx4_{j}"])
            em.idma(og_sb[j % 2][:], None, oslots.ap(),
                    IndirectOffsetOnAxis(idx4[j][:], axis=0),
                    reads=[f"idx4_{j}"] + osl_keys, writes=[f"og{j % 2}"])
            em.op("v", lambda v, j=j, k=k: v.tensor_add(
                ya_sb[k % 2][:], ya_sb[k % 2][:], og_sb[j % 2][:]),
                reads=[f"og{j % 2}"], writes=[f"ya{k % 2}"])
        em.dma("q", y_t[:, k:k + 1, :],
               ya_sb[k % 2][:].rearrange("p (k ch) -> p k ch", ch=D),
               reads=[f"ya{k % 2}"], writes=[f"yhbm_{k}"])
    em.final_wait()

    return nc



def make_inputs(c: CFG, core, xf, seq, wqkv, wgate, wout, gw, Wgg, Wuu, Wdd):
    n_own = c.n_own
    start = core * n_own
    x_own = np.ascontiguousarray(xf[start:start + n_own])
    b = start // seq
    q0 = start - b * seq
    npre = max(c.n_pre, 128)
    x_pre = np.zeros((npre, c.d), np.float32)
    x_pre[:q0] = xf[b * seq:b * seq + q0]
    pm = np.zeros((128, c.npc + c.noc), ml_dtypes.bfloat16)
    pm[:, c.npc:] = 1
    pm[:, :q0 // 128] = 1
    return {
        "x_own": x_own, "x_pre": x_pre,
        "wqkv": wqkv, "wgate": wgate, "wout": wout, "gw": gw,
        "Wg": Wgg, "Wu": Wuu, "Wd": Wdd,
        "pmask": pm,
        "ident": np.eye(128, dtype=ml_dtypes.bfloat16),
        "trimask": np.tile(np.triu(np.ones((128, 128), np.float32)), (1, 4)),
        "triuS": np.triu(np.ones((128, 128), ml_dtypes.bfloat16), 1),
        "iota": np.arange(128, dtype=np.float32).reshape(128, 1),
        "ub": np.ascontiguousarray(np.broadcast_to(
            (np.arange(c.E, dtype=np.float32) + 1.0) * c.cap, (128, c.E))),
    }


def kernel(L0_attn_x, L0_attn_w_qkv, L0_attn_w_gate, L0_attn_w_out,
           L0_ffn_gate_w, L0_ffn_Wg, L0_ffn_Wu, L0_ffn_Wd):
    c = CFG()
    B, T, D = L0_attn_x.shape
    xf = np.ascontiguousarray(np.asarray(L0_attn_x, dtype=np.float32)
                              .reshape(B * T, D))
    wqkv = np.ascontiguousarray(np.asarray(L0_attn_w_qkv, dtype=np.float32))
    wgate = np.ascontiguousarray(np.asarray(L0_attn_w_gate, dtype=np.float32))
    wout = np.ascontiguousarray(np.asarray(L0_attn_w_out, dtype=np.float32))
    gw = np.ascontiguousarray(np.asarray(L0_ffn_gate_w, dtype=np.float32))
    Wgg = np.ascontiguousarray(np.asarray(L0_ffn_Wg, dtype=np.float32))
    Wuu = np.ascontiguousarray(np.asarray(L0_ffn_Wu, dtype=np.float32))
    Wdd = np.ascontiguousarray(np.asarray(L0_ffn_Wd, dtype=np.float32))

    nc = build(c)
    in_maps = [make_inputs(c, core, xf, T, wqkv, wgate, wout, gw, Wgg, Wuu, Wdd)
               for core in range(8)]
    res = run_bass_kernel_spmd(nc, in_maps, core_ids=list(range(8)))
    out = np.concatenate([r["y"] for r in res.results], axis=0)
    return out.reshape(B, T, D)


# version 21
# speedup vs baseline: 1.0123x; 1.0123x over previous
import numpy as np
import ml_dtypes
from contextlib import ExitStack

import concourse.bass as bass
import concourse.mybir as mybir
from concourse.bass import IndirectOffsetOnAxis
from concourse.bass_utils import run_bass_kernel_spmd

F32 = mybir.dt.float32
I32 = mybir.dt.int32
BF16 = mybir.dt.bfloat16
AF = mybir.ActivationFunctionType
ALU = mybir.AluOpType
AX = mybir.AxisListType

EPS_LN = 1e-5
EPS_DEN = 1e-6
NEG_BIG = -1e30
BIGS = 32768.0
TOP_K = 4

ENGS = ("v", "s", "t", "q", "g")
DQS = ("q", "g", "s")


class CFG:
    def __init__(self, n_own_chunks=16, n_pre_chunks=48, n_experts=32,
                 d_inter=768, cap=384):
        self.d = 1024
        self.ncc = 8
        self.H = 16
        self.dh = 64
        self.noc = n_own_chunks
        self.npc = n_pre_chunks
        self.E = n_experts
        self.di = d_inter
        self.icc = d_inter // 128
        self.n_own = self.noc * 128
        self.n_pre = self.npc * 128
        self.cap = cap
        self.nsub = cap // 128
        self.nslot = n_experts * cap



class Em:
    def __init__(self, nc, ctx):
        self.nc = nc
        self.eng = {"v": nc.vector, "s": nc.scalar, "t": nc.tensor,
                    "q": nc.sync, "g": nc.gpsimd}
        self.sem = {e: ctx.enter_context(nc.semaphore(f"es_{e}")) for e in ENGS}
        self.NPOOL = 8
        self.dpool = {e: [ctx.enter_context(nc.semaphore(f"ds_{e}{i}"))
                          for i in range(self.NPOOL)] for e in DQS}
        self.duse = {e: [0] * self.NPOOL for e in DQS}
        self.ndma = {e: 0 for e in DQS}
        self.ops = {e: [] for e in ENGS}
        self.ninc = {e: 0 for e in ENGS}
        self.incpts = {e: [] for e in ENGS}
        self.clock = {e: {} for e in ENGS}
        self.lastw = {}
        self.readers = {}
        self.dmaclock = {}

    def _resolve(self, tok):
        if tok[0] == "d":
            _, qs, val = tok
            return ("d%s%d" % qs, val, self.dmaclock.get(tok, {}))
        _, e, idx = tok
        for oi, val in self.incpts[e]:
            if oi >= idx:
                return ("e" + e, val, self.ops[e][oi]["clk"])
        rec = self.ops[e][idx]
        if rec.get("isdma") or rec["inst"] is None:
            inst = self.eng[e].nop()
            self.ninc[e] += 1
            inst.then_inc(self.sem[e], 1)
            nidx = len(self.ops[e])
            self.ops[e].append({"inst": inst, "clk": dict(self.clock[e])})
            self.incpts[e].append((nidx, self.ninc[e]))
            return ("e" + e, self.ninc[e], self.ops[e][nidx]["clk"])
        self.ninc[e] += 1
        rec["inst"].then_inc(self.sem[e], 1)
        self.incpts[e].append((idx, self.ninc[e]))
        return ("e" + e, self.ninc[e], rec["clk"])

    def _semhandle(self, semkey):
        if semkey[0] == "e":
            return self.sem[semkey[1:]]
        return self.dpool[semkey[1]][int(semkey[2:])]

    def _pre(self, eng, reads, writes):
        toks = []
        for k in reads:
            t = self.lastw.get(k)
            if t is not None:
                toks.append(t)
        for k in writes:
            t = self.lastw.get(k)
            if t is not None:
                toks.append(t)
            toks.extend(self.readers.get(k, ()))
        best = {}
        for tok in toks:
            if tok[0] == "e" and tok[1] == "t" and eng == "t":
                continue
            semkey, val, snap = self._resolve(tok)
            if best.get(semkey, (0, None))[0] < val:
                best[semkey] = (val, snap)
        clk = self.clock[eng]
        for semkey, (val, snap) in sorted(best.items()):
            if clk.get(semkey, 0) >= val:
                continue
            self.eng[eng].wait_ge(self._semhandle(semkey), val)
            clk[semkey] = val
            for k2, v2 in snap.items():
                if clk.get(k2, 0) < v2:
                    clk[k2] = v2

    def _post(self, eng, tok, reads, writes):
        for k in writes:
            self.lastw[k] = tok
            self.readers[k] = []
        for k in reads:
            self.readers.setdefault(k, []).append(tok)

    def op(self, eng, fn, reads=(), writes=()):
        self._pre(eng, reads, writes)
        inst = fn(self.eng[eng])
        idx = len(self.ops[eng])
        self.ops[eng].append({"inst": inst, "clk": dict(self.clock[eng])})
        self._post(eng, ("e", eng, idx), reads, writes)
        return inst

    def _dma_issue(self, queue, fn, reads, writes):
        slot = self.ndma[queue] % self.NPOOL
        self.ndma[queue] += 1
        semkey = "d%s%d" % (queue, slot)
        prev = 16 * self.duse[queue][slot]
        clk = self.clock[queue]
        if prev and clk.get(semkey, 0) < prev:
            self.eng[queue].wait_ge(self.dpool[queue][slot], prev)
            clk[semkey] = prev
        self._pre(queue, reads, writes)
        inst = fn(self.eng[queue])
        self.duse[queue][slot] += 1
        val = 16 * self.duse[queue][slot]
        inst.then_inc(self.dpool[queue][slot], 16)
        self.ops[queue].append({"inst": inst, "clk": dict(self.clock[queue]),
                                "isdma": True})
        tok = ("d", (queue, slot), val)
        self.dmaclock[tok] = dict(self.clock[queue])
        self._post(queue, tok, reads, writes)
        return inst

    def dma(self, queue, out_ap, in_ap, reads=(), writes=(), **kw):
        return self._dma_issue(
            queue, lambda e: e.dma_start(out_ap, in_ap, **kw), reads, writes)

    def idma(self, out_ap, out_off, in_ap, in_off, reads=(), writes=()):
        return self._dma_issue(
            "g", lambda e: e.indirect_dma_start(out_ap, out_off, in_ap, in_off),
            reads, writes)

    def barrier(self):
        for e in ENGS:
            if e != "q" and self.ops[e]:
                self._resolve(("e", e, len(self.ops[e]) - 1))
        q = self.eng["q"]
        clk = self.clock["q"]
        for e in ENGS:
            if e == "q":
                continue
            val = self.ninc[e]
            if val and clk.get("e" + e, 0) < val:
                q.wait_ge(self.sem[e], val)
                clk["e" + e] = val
        for qq in DQS:
            for slot in range(self.NPOOL):
                val = 16 * self.duse[qq][slot]
                semkey = "d%s%d" % (qq, slot)
                if val and clk.get(semkey, 0) < val:
                    q.wait_ge(self.dpool[qq][slot], val)
                    clk[semkey] = val
        self.ninc["q"] += 1
        qval = self.ninc["q"]
        inst = q.nop()
        inst.then_inc(self.sem["q"], 1)
        self.ops["q"].append({"inst": inst, "clk": dict(clk)})
        self.incpts["q"].append((len(self.ops["q"]) - 1, qval))
        for e in ENGS:
            if e == "q":
                continue
            self.eng[e].wait_ge(self.sem["q"], qval)
            c2 = self.clock[e]
            c2["eq"] = max(c2.get("eq", 0), qval)
            for k2, v2 in clk.items():
                if c2.get(k2, 0) < v2:
                    c2[k2] = v2

    def final_wait(self):
        clk = self.clock["q"]
        for qq in DQS:
            for slot in range(self.NPOOL):
                val = 16 * self.duse[qq][slot]
                semkey = "d%s%d" % (qq, slot)
                if val and clk.get(semkey, 0) < val:
                    self.eng["q"].wait_ge(self.dpool[qq][slot], val)
                    clk[semkey] = val



def build(c: CFG):
    nc = bass.Bass("TRN2", target_bir_lowering=False)
    D, H, dh, ncc, icc = c.d, c.H, c.dh, c.ncc, c.icc
    NOC, NPC, E, DI = c.noc, c.npc, c.E, c.di
    C, NSUB = c.cap, c.nsub

    x_own = nc.dram_tensor("x_own", [c.n_own, D], F32, kind="ExternalInput")
    x_pre = nc.dram_tensor("x_pre", [max(c.n_pre, 128), D], F32, kind="ExternalInput")
    wqkv = nc.dram_tensor("wqkv", [D, 3 * D], F32, kind="ExternalInput")
    wgate = nc.dram_tensor("wgate", [D, D], F32, kind="ExternalInput")
    wout = nc.dram_tensor("wout", [D, D], F32, kind="ExternalInput")
    gw = nc.dram_tensor("gw", [E, D], F32, kind="ExternalInput")
    Wg = nc.dram_tensor("Wg", [E, D, DI], F32, kind="ExternalInput")
    Wu = nc.dram_tensor("Wu", [E, D, DI], F32, kind="ExternalInput")
    Wd = nc.dram_tensor("Wd", [E, DI, D], F32, kind="ExternalInput")
    pmask = nc.dram_tensor("pmask", [128, NPC + NOC], BF16, kind="ExternalInput")
    ident = nc.dram_tensor("ident", [128, 128], BF16, kind="ExternalInput")
    trimask = nc.dram_tensor("trimask", [128, 512], F32, kind="ExternalInput")
    triuS = nc.dram_tensor("triuS", [128, 128], BF16, kind="ExternalInput")
    iota = nc.dram_tensor("iota", [128, 1], F32, kind="ExternalInput")
    ub = nc.dram_tensor("ub", [128, E], F32, kind="ExternalInput")
    y = nc.dram_tensor("y", [c.n_own, D], F32, kind="ExternalOutput")
    y_part = nc.dram_tensor("y_part", [c.n_own, D], F32, kind="Internal")
    xn_spill = nc.dram_tensor("xn_spill", [c.n_own, D], BF16, kind="Internal")
    wslots = nc.dram_tensor("wslots", [c.nslot + 1, 2], F32, kind="Internal")
    oslots = nc.dram_tensor("oslots", [c.nslot + 1, D], BF16, kind="Internal")

    x_own_t = x_own.ap().rearrange("(k p) ch -> k p ch", p=128)
    x_pre_t = x_pre.ap().rearrange("(k p) ch -> k p ch", p=128)
    y_part_t = y_part.ap().rearrange("(k p) ch -> k p ch", p=128)
    xn_spill_t = xn_spill.ap().rearrange("(k p) ch -> k p ch", p=128)
    y_t = y.ap().rearrange("(k p) ch -> p k ch", p=128)

    Wg_r = Wg.ap().rearrange("e (r p) f -> e p r f", p=128)
    Wu_r = Wu.ap().rearrange("e (r p) f -> e p r f", p=128)
    Wd_r = Wd.ap().rearrange("e (r p) f -> e p r f", p=128)

    ctx = ExitStack()
    em = Em(nc, ctx)
    sb = lambda name, shape, dt: ctx.enter_context(nc.sbuf_tensor(name, shape, dt))

    ident_b = sb("ident_b", [128, 128], BF16)
    maskT = sb("maskT", [128, 512], F32)
    hstat = sb("hstat", [128, 32], F32)
    pmask_b = sb("pmask_b", [128, NPC + NOC], BF16)
    gw_bf = sb("gw_bf", [E, D], BF16)
    gwT_b = sb("gwT_b", [128, ncc * E], BF16)
    combine = sb("combine", [128, NOC * E], F32)
    xc = [sb(f"xc{i}", [128, D], F32) for i in range(3)]
    stat = sb("stat", [128, 12], F32)
    eps_t = sb("eps_t", [128, 1], F32)
    lg = sb("lg", [128, max(E, 8)], F32)
    cur = sb("cur", [128, max(E, 8)], F32)
    mx8 = sb("mx8", [128, 8], F32)
    mexp = sb("mexp", [128, max(E, 8)], F32)
    ksum_sb = sb("ksum_sb", [128, ncc], F32)
    trash = sb("trash", [128, D], F32)

    triuS_b = sb("triuS_b", [128, 128], BF16)
    ones_bf = sb("ones_bf", [128, 128], BF16)
    ones_f32 = sb("ones_f32", [128, 128], F32)
    iota_b = sb("iota_b", [128, 1], F32)
    ub_sb = sb("ub_sb", [128, E], F32)
    offs_sb = sb("offs_sb", [128, E], F32)
    mask_bf = sb("mask_bf", [128, E], BF16)
    rank_sb = sb("rank_sb", [128, E], F32)
    slot_sb = sb("slot_sb", [128, E], F32)
    okm = sb("okm", [128, E], F32)
    mask2 = sb("mask2", [128, E], F32)
    neg_sb = sb("neg_sb", [128, E], F32)
    ind_sb = sb("ind_sb", [128, E], F32)
    mx8w = sb("mx8w", [128, 8], F32)
    slots4 = sb("slots4", [128, NOC * TOP_K], F32)
    pairs_sb = [sb(f"pairs{j}", [128, 2], F32) for j in range(TOP_K)]
    idx4 = [sb(f"idx4_{j}", [128, 1], I32) for j in range(TOP_K)]

    ctx_attn = ExitStack()
    psA = lambda name, shape, dt: ctx_attn.enter_context(nc.psum_tensor(name, shape, dt))
    S_ps = psA("S_ps", [128, 512], F32)
    tr_ps = psA("tr_ps", [128, 1024], BF16)
    pq = [psA(f"pq{i}", [128, 512], F32) for i in range(4)]
    scp = [psA(f"sc{i}", [128, 512], F32) for i in range(2)]

    ga = ExitStack()
    sba = lambda name, shape, dt: ga.enter_context(nc.sbuf_tensor(name, shape, dt))
    wqkv_b = sba("wqkv_b", [128, ncc * 3 * D], BF16)
    wgate_b = sba("wgate_b", [128, ncc * D], BF16)
    xnT_pre = sba("xnT_pre", [128, D], BF16)
    xnT_c = sba("xnT_c", [128, ncc * 128], BF16)
    xn_b = sba("xn_b", [128, D], BF16)
    qk_sb = sba("qk_sb", [128, 2 * D], BF16)
    qkTe = sba("qkTe", [128, 2 * D], BF16)
    Vb = sba("Vb", [128, H * (dh + 1)], BF16)
    gate_sb = sba("gate_sb", [128, D], BF16)
    tmin = sba("tmin", [128, 2 * D], F32)
    texp = sba("texp", [128, 2 * D], F32)
    S0b = sba("S0b", [128, (H // 2) * (dh + 1)], BF16)
    msc4 = [sba(f"msc4_{i}", [128, 512], BF16) for i in range(2)]
    oall = sba("oall", [128, H * (dh + 1)], F32)
    zeros_b = sba("zeros_b", [128, 512], BF16)
    z2_sb = sba("z2_sb", [128, 2 * (c.nslot // 128)], F32)
    o_sb = [sba(f"o_sb{i}", [128, D], BF16) for i in range(2)]
    oT_sb = sba("oT_sb", [128, D], BF16)
    yp_sb = [sba(f"yp_sb{i}", [128, D], F32) for i in range(2)]
    wout_b = sba("wout_b", [128, ncc * D], BF16)

    VV = Vb[:].rearrange("p (h cc) -> p h cc", cc=dh + 1)
    S0v = S0b[:].rearrange("p (h cc) -> p h cc", cc=dh + 1)
    qkTe_v = qkTe[:].rearrange("p (j t) -> p j t", t=128)
    xnT_cv = xnT_c[:].rearrange("p (j t) -> p j t", t=128)

    em.dma("q", ident_b[:], ident.ap(), writes=["ident_b"])
    em.dma("q", maskT[:], trimask.ap(), writes=["maskT"])
    em.dma("q", pmask_b[:], pmask.ap(), writes=["pmask_b"])
    em.dma("q", triuS_b[:], triuS.ap(), writes=["triuS_b"])
    em.dma("q", iota_b[:], iota.ap(), writes=["iota_b"])
    em.dma("q", ub_sb[:], ub.ap(), writes=["ub_sb"])
    em.dma("g", gw_bf[:], gw.ap(), writes=["gw_bf"])
    em.dma("g", wqkv_b[:].rearrange("p (r f) -> p r f", f=3 * D),
           wqkv.ap().rearrange("(r p) f -> p r f", p=128), writes=["wqkv_b"])
    em.dma("g", wgate_b[:].rearrange("p (r f) -> p r f", f=D),
           wgate.ap().rearrange("(r p) f -> p r f", p=128), writes=["wgate_b"])
    em.dma("g", wout_b[:].rearrange("p (r f) -> p r f", f=D),
           wout.ap().rearrange("(r p) f -> p r f", p=128), writes=["wout_b"])
    em.op("v", lambda v: v.memset(VV[:, :, dh:dh + 1], 1.0), writes=["Vb"])
    em.op("v", lambda v: v.memset(ksum_sb[:], 0.0), writes=["ksum_sb"])
    em.op("v", lambda v: v.memset(eps_t[:], EPS_LN), writes=["eps_t"])
    em.op("v", lambda v: v.memset(ones_bf[:], 1.0), writes=["ones_bf"])
    em.op("v", lambda v: v.memset(ones_f32[:], 1.0), writes=["ones_f32"])
    em.op("v", lambda v: v.memset(z2_sb[:], 0.0), writes=["z2_sb"])
    em.op("v", lambda v: v.tensor_scalar_add(offs_sb[0:1, :], ub_sb[0:1, :],
                                             float(-C)),
          reads=["ub_sb"], writes=["offs"])
    em.dma("g", wslots.ap()[0:c.nslot, :].rearrange("(p s) cc -> p s cc", p=128),
           z2_sb[:].rearrange("p (s cc) -> p s cc", cc=2),
           reads=["z2_sb"], writes=["wsl_init"])
    em.dma("g", wslots.ap()[c.nslot:c.nslot + 1, :], z2_sb[0:1, 0:2],
           reads=["z2_sb"], writes=["wsl_init2"])

    for j in range(ncc):
        em.op("t", lambda t, j=j: t.transpose(
            tr_ps[:, j * E:(j + 1) * E], gw_bf[0:E, j * 128:(j + 1) * 128],
            ident_b[0:E, 0:E]),
            reads=["gw_bf", "ident_b"], writes=["tr_ps"])
    em.op("v", lambda v: v.tensor_copy(gwT_b[:], tr_ps[:, 0:ncc * E]),
          reads=["tr_ps"], writes=["gwT_b"])
    em.op("v", lambda v: v.memset(zeros_b[:], 0.0), writes=["zeros_b"])
    em.op("t", lambda t: t.matmul(S_ps[:], zeros_b[:, 0:128], zeros_b[:],
                                  start=True, stop=False,
                                  skip_group_check=True),
          reads=["zeros_b"], writes=["S_ps"])

    def layernorm(xbuf, xkey):
        em.op("v", lambda v: v.reduce_sum(stat[:, 0:1], xbuf[:], axis=AX.X),
              reads=[xkey], writes=["st_s"])
        em.op("v", lambda v: v.tensor_scalar_mul(stat[:, 0:1], stat[:, 0:1], 1.0 / D),
              reads=["st_s"], writes=["st_m"])
        em.op("s", lambda s: s.activation(trash[:], xbuf[:], AF.Square,
                                          accum_out=stat[:, 1:2]),
              reads=[xkey], writes=["trash", "st_ssq"])
        em.op("v", lambda v: v.tensor_scalar(stat[:, 2:3], stat[:, 0:1],
                                             stat[:, 0:1], -1.0,
                                             op0=ALU.mult, op1=ALU.mult),
              reads=["st_m"], writes=["st_nb"])
        em.op("v", lambda v: v.tensor_add(stat[:, 2:3], stat[:, 2:3], eps_t[:]),
              reads=["st_nb", "eps_t"], writes=["st_nb"])
        em.op("s", lambda s: s.activation(stat[:, 3:4], stat[:, 1:2], AF.Ln,
                                          bias=stat[:, 2:3], scale=1.0 / D),
              reads=["st_ssq", "st_nb"], writes=["st_tln"])
        em.op("s", lambda s: s.activation(stat[:, 4:5], stat[:, 3:4], AF.Exp,
                                          scale=-0.5),
              reads=["st_tln"], writes=["st_rstd"])
        em.op("v", lambda v: v.tensor_scalar(xn_b[:], xbuf[:], stat[:, 0:1],
                                             stat[:, 4:5],
                                             op0=ALU.subtract, op1=ALU.mult),
              reads=[xkey, "st_m", "st_rstd"], writes=["xn_b"])

    def transpose8(src_ap, src_key, dst_ap, dst_key, bank, bank_key, ident_ap,
                   eng="v"):
        for j in range(ncc):
            em.op("t", lambda t, j=j: t.transpose(
                bank[:, j * 128:(j + 1) * 128],
                src_ap[:, j * 128:(j + 1) * 128], ident_ap),
                reads=[src_key, "ident_b"], writes=[bank_key])
        bank3 = bank[:].rearrange("p (j t) -> p j t", t=128)
        if eng == "v":
            em.op("v", lambda v: v.tensor_copy(dst_ap, bank3[:, 0:ncc, :]),
                  reads=[bank_key], writes=[dst_key])
        else:
            em.op("s", lambda s: s.copy(dst_ap, bank3[:, 0:ncc, :]),
                  reads=[bank_key], writes=[dst_key])

    def elu1(psrc_ap, pkey, toff, dst_ap, dst_key):
        em.op("v", lambda v: v.tensor_scalar_min(tmin[:, toff:toff + 512],
                                                 psrc_ap, 0.0),
              reads=[pkey], writes=["tmin"])
        em.op("s", lambda s: s.activation(texp[:, toff:toff + 512],
                                          tmin[:, toff:toff + 512], AF.Exp),
              reads=["tmin"], writes=["texp"])
        em.op("v", lambda v: v.scalar_tensor_tensor(
            dst_ap, psrc_ap, 0.0, texp[:, toff:toff + 512],
            op0=ALU.max, op1=ALU.add),
            reads=[pkey, "texp"], writes=[dst_key])

    def ksum_update(chunk_col):
        for b in range(ncc):
            em.op("t", lambda t, b=b: t.matmul(
                scp[1][:, b:b + 1], qk_sb[:, D + b * 128:D + (b + 1) * 128],
                pmask_b[:, chunk_col:chunk_col + 1], start=True, stop=True),
                reads=["qk_sb", "pmask_b"], writes=["sc1"])
        em.op("v", lambda v: v.tensor_add(ksum_sb[:], ksum_sb[:], scp[1][:, 0:ncc]),
              reads=["sc1", "ksum_sb"], writes=["ksum_sb"])

    def s_update():
        for h in range(H):
            em.op("t", lambda t, h=h: t.matmul(
                S_ps[64 * (h % 2):64 * (h % 2) + 64,
                     dh * (h // 2):dh * (h // 2) + dh],
                qk_sb[:, D + dh * h:D + dh * (h + 1)],
                VV[:, h, 0:dh],
                start=False, stop=False, skip_group_check=True),
                reads=["qk_sb", "Vb"], writes=["S_ps"])

    def kv_state_update(chunk_col):
        ksum_update(chunk_col)
        s_update()

    if NPC > 0:
        em.dma("q", xc[0][:], x_pre_t[0, :, :], writes=["xc0"])
    elif NOC > 0:
        em.dma("q", xc[0][:], x_own_t[0, :, :], writes=["xc0"])
    if NPC > 0:
        layernorm(xc[0], "xc0")
        transpose8(xn_b[:], "xn_b",
                   xnT_pre[:].rearrange("p (j t) -> p j t", t=128), "xnT_pre",
                   tr_ps, "tr_ps", ident_b[:], eng="s")
    for k in range(NPC):
        if k + 1 < NPC:
            em.dma("q", xc[(k + 1) % 3][:], x_pre_t[k + 1, :, :],
                   writes=[f"xc{(k + 1) % 3}"])
        elif NOC > 0:
            em.dma("q", xc[(NPC) % 3][:], x_own_t[0, :, :],
                   writes=[f"xc{NPC % 3}"])
        for b in range(4):
            for j in range(ncc):
                em.op("t", lambda t, j=j, b=b: t.matmul(
                    pq[b][:], xnT_pre[:, j * 128:(j + 1) * 128],
                    wqkv_b[:, j * 3 * D + D + b * 512:j * 3 * D + D + (b + 1) * 512],
                    start=(j == 0), stop=(j == ncc - 1)),
                    reads=["xnT_pre", "wqkv_b"], writes=[f"pq{b}"])
        elu1(pq[0][:], "pq0", 0, qk_sb[:, D:D + 512], "qk_sb")
        elu1(pq[1][:], "pq1", 512, qk_sb[:, D + 512:2 * D], "qk_sb")
        em.op("s", lambda s: s.copy(
            VV[:, 0:8, 0:dh], pq[2][:].rearrange("p (h cc) -> p h cc", cc=dh)),
            reads=["pq2"], writes=["Vb"])
        em.op("s", lambda s: s.copy(
            VV[:, 8:16, 0:dh], pq[3][:].rearrange("p (h cc) -> p h cc", cc=dh)),
            reads=["pq3"], writes=["Vb"])
        if k + 1 < NPC:
            layernorm(xc[(k + 1) % 3], f"xc{(k + 1) % 3}")
            transpose8(xn_b[:], "xn_b",
                       xnT_pre[:].rearrange("p (j t) -> p j t", t=128), "xnT_pre",
                       tr_ps, "tr_ps", ident_b[:], eng="s")
        kv_state_update(k)

    wsl_keys = ["wsl_init"]
    xnsp_keys = []

    def own_ln(k):
        par = (NPC + k) % 3
        layernorm(xc[par], f"xc{par}")
        em.dma("q", xn_spill_t[k, :, :], xn_b[:], reads=["xn_b"],
               writes=[f"xnsp_{k}"])
        xnsp_keys.append(f"xnsp_{k}")
        transpose8(xn_b[:], "xn_b", xnT_cv, "xnT_c", tr_ps, "tr_ps", ident_b[:],
                   eng="s")

    if NOC > 0:
        own_ln(0)
    for k in range(NOC):
        if k + 1 < NOC:
            em.dma("q", xc[(NPC + k + 1) % 3][:], x_own_t[k + 1, :, :],
                   writes=[f"xc{(NPC + k + 1) % 3}"])

        def xl(j):
            return xnT_c[:, j * 128:(j + 1) * 128]

        for j in range(ncc):
            em.op("t", lambda t, j=j: t.matmul(
                scp[0][:, 0:E], xl(j), gwT_b[:, j * E:(j + 1) * E],
                start=(j == 0), stop=(j == ncc - 1)),
                reads=["xnT_c", "gwT_b"], writes=["sc0"])
        em.op("v", lambda v: v.tensor_copy(lg[:, 0:E], scp[0][:, 0:E]),
              reads=["sc0"], writes=["lg"])
        em.op("v", lambda v: v.max(out=mx8[:], in_=lg[:, 0:E]),
              reads=["lg"], writes=["mx8"])
        em.op("v", lambda v: v.memset(mx8[:, TOP_K:8], NEG_BIG), writes=["mx8"])
        em.op("v", lambda v: v.match_replace(cur[:, 0:E], mx8[:], lg[:, 0:E],
                                             NEG_BIG),
              reads=["mx8", "lg"], writes=["cur"])
        em.op("v", lambda v: v.tensor_single_scalar(cur[:, 0:E], cur[:, 0:E],
                                                    -1e20, ALU.is_lt),
              reads=["cur"], writes=["cur"])
        em.op("v", lambda v: v.reduce_max(stat[:, 5:6], lg[:, 0:E], axis=AX.X,
                                          negate=True),
              reads=["lg"], writes=["st_negm"])
        em.op("s", lambda s: s.activation(mexp[:, 0:E], lg[:, 0:E], AF.Exp,
                                          bias=stat[:, 5:6]),
              reads=["lg", "st_negm"], writes=["mexp"])
        em.op("v", lambda v: v.tensor_mul(mexp[:, 0:E], mexp[:, 0:E], cur[:, 0:E]),
              reads=["mexp", "cur"], writes=["mexp"])
        em.op("v", lambda v: v.reduce_sum(stat[:, 6:7], mexp[:, 0:E], axis=AX.X),
              reads=["mexp"], writes=["st_den"])
        em.op("v", lambda v: v.reciprocal(stat[:, 7:8], stat[:, 6:7]),
              reads=["st_den"], writes=["st_rden"])
        em.op("v", lambda v, k=k: v.tensor_scalar_mul(
            combine[:, k * E:(k + 1) * E], mexp[:, 0:E], stat[:, 7:8]),
            reads=["mexp", "st_rden"], writes=[f"combine_{k}"])

        for b in range(4):
            for j in range(ncc):
                em.op("t", lambda t, j=j, b=b: t.matmul(
                    pq[b][:], xl(j),
                    wqkv_b[:, j * 3 * D + b * 512:j * 3 * D + (b + 1) * 512],
                    start=(j == 0), stop=(j == ncc - 1)),
                    reads=["xnT_c", "wqkv_b"], writes=[f"pq{b}"])
        for b in range(4):
            elu1(pq[b][:], f"pq{b}", b * 512,
                 qk_sb[:, b * 512:(b + 1) * 512], "qk_sb")
        transpose8(qk_sb[:, 0:D], "qk_sb", qkTe_v[:, 0:8, :], "qkTe",
                   tr_ps, "tr_ps", ident_b[:], eng="s")
        transpose8(qk_sb[:, D:2 * D], "qk_sb", qkTe_v[:, 8:16, :], "qkTe",
                   tr_ps, "tr_ps", ident_b[:], eng="s")


        em.op("s", lambda s: s.copy(
            S0v[:, :, dh:dh + 1],
            ksum_sb[:].rearrange("p (j o) -> p j o", o=1)),
            reads=["ksum_sb"], writes=["S0b"])
        ksum_update(NPC + k)
        for b in range(4):
            for j in range(ncc):
                if b < 2:
                    wsrc, wkey = wqkv_b, "wqkv_b"
                    wcol = j * 3 * D + 2 * D + b * 512
                else:
                    wsrc, wkey = wgate_b, "wgate_b"
                    wcol = j * D + (b - 2) * 512
                em.op("t", lambda t, j=j, b=b, wsrc=wsrc, wcol=wcol: t.matmul(
                    pq[b][:], xl(j), wsrc[:, wcol:wcol + 512],
                    start=(j == 0), stop=(j == ncc - 1)),
                    reads=["xnT_c", wkey], writes=[f"pq{b}"])
        em.op("s", lambda s: s.copy(
            VV[:, 0:8, 0:dh], pq[0][:].rearrange("p (h cc) -> p h cc", cc=dh)),
            reads=["pq0"], writes=["Vb"])
        em.op("s", lambda s: s.copy(
            VV[:, 8:16, 0:dh], pq[1][:].rearrange("p (h cc) -> p h cc", cc=dh)),
            reads=["pq1"], writes=["Vb"])
        em.op("s", lambda s: s.activation(gate_sb[:, 0:512], pq[2][:], AF.Sigmoid),
              reads=["pq2"], writes=["gate_sb"])
        em.op("s", lambda s: s.activation(gate_sb[:, 512:D], pq[3][:], AF.Sigmoid),
              reads=["pq3"], writes=["gate_sb"])


        if k + 1 < NOC:
            own_ln(k + 1)
        em.op("s", lambda s: s.copy(
            S0v[:, :, 0:dh], S_ps[:].rearrange("p (h cc) -> p h cc", cc=dh)),
            reads=["S_ps"], writes=["S0b"])
        s_update()

        def tr8o(kk):
            transpose8(o_sb[kk % 2][:], f"o_sb{kk % 2}",
                       oT_sb[:].rearrange("p (j t) -> p j t", t=128),
                       "oT_sb", tr_ps, "tr_ps", ident_b[:], eng="s")

        def outproj_mm(idx):
            half, j = idx // ncc, idx % ncc
            em.op("t", lambda t, j=j, half=half: t.matmul(
                pq[2 + half][:, 0:512],
                oT_sb[:, j * 128:(j + 1) * 128],
                wout_b[:, j * D + half * 512:j * D + (half + 1) * 512],
                start=(j == 0), stop=(j == ncc - 1)),
                reads=["oT_sb", "wout_b"], writes=[f"pq{2 + half}"])

        def yp_store(kk):
            parp = (NPC + kk) % 3
            for half in range(2):
                em.op("v", lambda v, half=half, parp=parp, kk=kk: v.tensor_add(
                    yp_sb[kk % 2][:, half * 512:(half + 1) * 512],
                    pq[2 + half][:, 0:512],
                    xc[parp][:, half * 512:(half + 1) * 512]),
                    reads=[f"pq{2 + half}", f"xc{parp}"], writes=[f"yp{kk % 2}"])
            em.dma("q", y_part_t[kk, :, :], yp_sb[kk % 2][:],
                   reads=[f"yp{kk % 2}"], writes=[f"ypart_{kk}"])

        if k > 0:
            tr8o(k - 1)

        o_cur, o_key = o_sb[k % 2], f"o_sb{k % 2}"

        def score(h):
            po = 64 * (h % 2)
            ft_q, ft_k = h // 2, 8 + h // 2
            sbank, skey = scp[h % 2], f"sc{h % 2}"
            mt, mi = (h % 8) // 4, h % 4
            em.op("t", lambda t, po=po, ft_k=ft_k, ft_q=ft_q, sbank=sbank: t.matmul(
                sbank[:, 0:128],
                qkTe[po:po + 64, ft_k * 128:ft_k * 128 + 128],
                qkTe[po:po + 64, ft_q * 128:ft_q * 128 + 128],
                start=True, stop=True),
                reads=["qkTe"], writes=[skey])
            em.op("v", lambda v, mt=mt, mi=mi, sbank=sbank: v.tensor_mul(
                msc4[mt][:, mi * 128:(mi + 1) * 128], sbank[:, 0:128],
                maskT[:, 0:128]),
                reads=[skey, "maskT"], writes=[f"msc4_{mt}_{mi}"])

        score(0)
        score(1)
        op_idx = 0
        for h in range(H):
            po = 64 * (h % 2)
            ft_q = h // 2
            obank, okey = pq[h % 2], f"pq{h % 2}"
            mt, mi = (h % 8) // 4, h % 4
            em.op("t", lambda t, h=h, mt=mt, mi=mi, obank=obank: t.matmul(
                obank[:, 0:dh + 1], msc4[mt][:, mi * 128:(mi + 1) * 128],
                Vb[:, h * (dh + 1):(h + 1) * (dh + 1)],
                start=True, stop=False),
                reads=[f"msc4_{mt}_{mi}", "Vb"], writes=[okey])
            em.op("t", lambda t, h=h, po=po, ft_q=ft_q, obank=obank: t.matmul(
                obank[:, 0:dh + 1],
                qkTe[po:po + 64, ft_q * 128:ft_q * 128 + 128],
                S0b[po:po + 64, (dh + 1) * (h // 2):(dh + 1) * (h // 2 + 1)],
                start=False, stop=True),
                reads=["qkTe", "S0b"], writes=[okey])
            if h + 2 < H:
                score(h + 2)
            if k > 0 and h % 2 == 1:
                outproj_mm(op_idx)
                outproj_mm(op_idx + 1)
                op_idx += 2
            em.op("s", lambda s, h=h, obank=obank: s.copy(
                oall[:, h * (dh + 1):(h + 1) * (dh + 1)], obank[:, 0:dh + 1]),
                reads=[okey], writes=[f"oall{h // 8}"])
            if h % 8 == 7:
                hf = h // 8
                oallv = oall[:].rearrange("p (hh cc) -> p hh cc", cc=dh + 1)
                em.op("v", lambda v, hf=hf, oallv=oallv: v.tensor_scalar_add(
                    hstat[:, hf * 8:hf * 8 + 8].rearrange(
                        "p (hh o) -> p hh o", o=1),
                    oallv[:, hf * 8:hf * 8 + 8, dh:dh + 1], EPS_DEN),
                    reads=[f"oall{hf}"], writes=[f"hdn{hf}"])
                em.op("v", lambda v, hf=hf: v.reciprocal(
                    hstat[:, 16 + hf * 8:24 + hf * 8],
                    hstat[:, hf * 8:hf * 8 + 8]),
                    reads=[f"hdn{hf}"], writes=[f"hrec{hf}"])
                for h2 in range(hf * 8, hf * 8 + 8):
                    em.op("v", lambda v, h2=h2, hf=hf: v.scalar_tensor_tensor(
                        o_cur[:, dh * h2:dh * (h2 + 1)],
                        oall[:, h2 * (dh + 1):h2 * (dh + 1) + dh],
                        hstat[:, 16 + h2:17 + h2],
                        gate_sb[:, dh * h2:dh * (h2 + 1)],
                        op0=ALU.mult, op1=ALU.mult),
                        reads=[f"oall{hf}", f"hrec{hf}", "gate_sb"],
                        writes=[o_key])
        if k > 0:
            yp_store(k - 1)
        em.op("v", lambda v: v.tensor_copy(mask_bf[:], cur[:, 0:E]),
              reads=["cur"], writes=["mask_bf"])
        em.op("t", lambda t: t.matmul(scp[1][:, 128:160], triuS_b[:], mask_bf[:],
                                      start=True, stop=True),
              reads=["triuS_b", "mask_bf"], writes=["sc1"])
        em.op("t", lambda t: t.matmul(scp[1][:, 160:192], ones_f32[0:1, :],
                                      offs_sb[0:1, :], start=True, stop=True),
              reads=["ones_f32", "offs"], writes=["sc1"])
        em.op("t", lambda t: t.matmul(scp[1][0:1, 192:224], ones_bf[:, 0:1],
                                      mask_bf[:], start=True, stop=True),
              reads=["ones_bf", "mask_bf"], writes=["sc1"])
        em.op("v", lambda v: v.tensor_copy(rank_sb[:], scp[1][:, 128:160]),
              reads=["sc1"], writes=["rank_sb"])
        em.op("v", lambda v: v.tensor_add(slot_sb[:], rank_sb[:],
                                          scp[1][:, 160:192]),
              reads=["rank_sb", "sc1"], writes=["slot_sb"])
        em.op("v", lambda v: v.tensor_add(offs_sb[0:1, :], offs_sb[0:1, :],
                                          scp[1][0:1, 192:224]),
              reads=["offs", "sc1"], writes=["offs"])
        em.op("v", lambda v: v.tensor_tensor(okm[:], slot_sb[:], ub_sb[:],
                                             op=ALU.is_lt),
              reads=["slot_sb", "ub_sb"], writes=["okm"])
        em.op("v", lambda v: v.tensor_mul(mask2[:], okm[:], cur[:, 0:E]),
              reads=["okm", "cur"], writes=["mask2"])
        em.op("v", lambda v: v.tensor_scalar_add(slot_sb[:], slot_sb[:], -BIGS),
              reads=["slot_sb"], writes=["slot_sb"])
        em.op("v", lambda v: v.tensor_mul(slot_sb[:], slot_sb[:], mask2[:]),
              reads=["slot_sb", "mask2"], writes=["slot_sb"])
        em.op("v", lambda v: v.tensor_scalar_add(slot_sb[:], slot_sb[:], BIGS),
              reads=["slot_sb"], writes=["slot_sb"])
        em.op("v", lambda v: v.tensor_scalar(neg_sb[:], slot_sb[:], -1.0, BIGS,
                                             op0=ALU.mult, op1=ALU.add),
              reads=["slot_sb"], writes=["neg_sb"])
        em.op("v", lambda v: v.max(out=mx8w[:], in_=neg_sb[:]),
              reads=["neg_sb"], writes=["mx8w"])
        em.op("v", lambda v, k=k: v.tensor_scalar(
            slots4[:, 4 * k:4 * k + 4], mx8w[:, 0:4], -1.0, BIGS,
            op0=ALU.mult, op1=ALU.add),
            reads=["mx8w"], writes=["slots4"])
        em.op("v", lambda v, k=k: v.tensor_scalar_min(
            slots4[:, 4 * k:4 * k + 4], slots4[:, 4 * k:4 * k + 4],
            float(c.nslot)),
            reads=["slots4"], writes=["slots4"])
        for j in range(TOP_K):
            em.op("v", lambda v, j=j: v.tensor_single_scalar(
                ind_sb[:], neg_sb[:], mx8w[:, j:j + 1], ALU.is_equal),
                reads=["neg_sb", "mx8w"], writes=["ind_sb"])
            em.op("v", lambda v, j=j, k=k: v.tensor_mul(
                ind_sb[:], ind_sb[:], combine[:, k * E:(k + 1) * E]),
                reads=["ind_sb", f"combine_{k}"], writes=["ind_sb"])
            em.op("v", lambda v, j=j: v.reduce_sum(
                pairs_sb[j][:, 1:2], ind_sb[:], axis=AX.X),
                reads=["ind_sb"], writes=[f"pairs{j}"])
            em.op("v", lambda v, j=j, k=k: v.tensor_scalar_add(
                pairs_sb[j][:, 0:1], iota_b[:], float(k * 128)),
                reads=["iota_b"], writes=[f"pairs{j}"])
            em.op("v", lambda v, j=j, k=k: v.tensor_copy(
                idx4[j][:], slots4[:, 4 * k + j:4 * k + j + 1]),
                reads=["slots4"], writes=[f"idx4_{j}"])
            em.idma(wslots.ap(), IndirectOffsetOnAxis(idx4[j][:], axis=0),
                    pairs_sb[j][:], None,
                    reads=[f"idx4_{j}", f"pairs{j}", "wsl_init", "wsl_init2"],
                    writes=[f"wsl_{k}_{j}"])
            wsl_keys.append(f"wsl_{k}_{j}")


    if NOC > 0:
        tr8o(NOC - 1)
        for idx in range(2 * ncc):
            outproj_mm(idx)
        yp_store(NOC - 1)

    em.barrier()
    ga.close()
    ctx_attn.close()

    gc = ExitStack()
    sbc = lambda name, shape, dt: gc.enter_context(nc.sbuf_tensor(name, shape, dt))
    NR = 3
    Wg_b = [sbc(f"Wg_b{i}", [128, ncc * DI], BF16) for i in range(NR)]
    Wu_b = [sbc(f"Wu_b{i}", [128, ncc * DI], BF16) for i in range(NR)]
    Wd_b = [sbc(f"Wd_b{i}", [128, icc * D], BF16) for i in range(NR)]
    pe_sb = [sbc(f"pe{i}", [128, 2 * NSUB], F32) for i in range(NR)]
    ie_sb = [[sbc(f"ie{i}_{s}", [128, 1], I32) for s in range(NSUB)]
             for i in range(NR)]
    xg_sb = [[sbc(f"xg{i}_{s}", [128, D], BF16) for s in range(NSUB)]
             for i in range(2)]
    xgT = [sbc(f"xgT{i}", [128, ncc * C], BF16) for i in range(2)]
    aT = sbc("aT", [128, icc * C], BF16)
    sg = [sbc(f"sg{i}", [128, C], F32) for i in range(2)]
    os_sb = [sbc(f"os_sb{i}", [128, D], BF16) for i in range(2)]
    og_sb = [sbc(f"og{i}", [128, D], BF16) for i in range(2)]
    ctx_c = ExitStack()
    gq = [ctx_c.enter_context(nc.psum_tensor(f"gq{i}", [128, C], F32))
          for i in range(2)]
    uq = [ctx_c.enter_context(nc.psum_tensor(f"uq{i}", [128, C], F32))
          for i in range(2)]
    wd_ps = ctx_c.enter_context(nc.psum_tensor("wdp", [128, 1024], F32))
    trF = ctx_c.enter_context(nc.psum_tensor("trF", [128, 1024], BF16))

    xgT_v = [xgT[i][:].rearrange("p (j t) -> p j t", t=C) for i in range(2)]
    osl_keys = ["osl_z"]

    em.op("v", lambda v: v.memset(os_sb[0][:], 0.0), writes=["os0"])
    em.dma("s", oslots.ap()[c.nslot:c.nslot + 1, :], os_sb[0][0:1, :],
           reads=["os0"], writes=["osl_z"])

    def load_expert(e):
        par = e % NR
        em.dma("g", Wg_b[par][:].rearrange("p (r f) -> p r f", f=DI),
               Wg_r[e, :, :, :], writes=[f"Wg{par}"])
        em.dma("g", Wu_b[par][:].rearrange("p (r f) -> p r f", f=DI),
               Wu_r[e, :, :, :], writes=[f"Wu{par}"])
        em.dma("g", Wd_b[par][:].rearrange("p (r f) -> p r f", f=D),
               Wd_r[e, :, :, :], writes=[f"Wd{par}"])
        em.dma("q", pe_sb[par][:].rearrange("p (s cc) -> p s cc", cc=2),
               wslots.ap()[e * C:(e + 1) * C, :].rearrange(
                   "(s p) cc -> p s cc", p=128),
               reads=wsl_keys, writes=[f"pe{par}"])

    def gather_stage(e):
        par = e % NR
        xpar = e % 2
        pv = pe_sb[par][:].rearrange("p (s cc) -> p s cc", cc=2)
        for s in range(NSUB):
            em.op("v", lambda v, s=s, par=par: v.tensor_copy(
                ie_sb[par][s][:], pv[:, s, 0:1]),
                reads=[f"pe{par}"], writes=[f"ie{par}{s}"])
            em.idma(xg_sb[xpar][s][:], None, xn_spill.ap(),
                    IndirectOffsetOnAxis(ie_sb[par][s][:], axis=0),
                    reads=[f"ie{par}{s}"] + xnsp_keys, writes=[f"xg{xpar}{s}"])

    def trans_stage(e):
        par = e % 2
        for s in range(NSUB):
            for j in range(ncc):
                em.op("t", lambda t, j=j, s=s, par=par: t.transpose(
                    trF[:, j * 128:(j + 1) * 128],
                    xg_sb[par][s][:, j * 128:(j + 1) * 128], ident_b[:]),
                    reads=[f"xg{par}{s}", "ident_b"], writes=["trF"])
            em.op("s", lambda s_, s=s, par=par: s_.copy(
                xgT_v[par][:, :, s * 128:(s + 1) * 128],
                trF[:].rearrange("p (j t) -> p j t", t=128)),
                reads=["trF"], writes=[f"xgT{par}"])

    def gu_stage(e):
        par = e % NR
        xpar = e % 2
        for ic in range(icc):
            w = ic % 2
            for j in range(ncc):
                em.op("t", lambda t, j=j, ic=ic, w=w, par=par, xpar=xpar: t.matmul(
                    gq[w][:, 0:C],
                    Wg_b[par][:, j * DI + ic * 128:j * DI + (ic + 1) * 128],
                    xgT[xpar][:, j * C:(j + 1) * C],
                    start=(j == 0), stop=(j == ncc - 1)),
                    reads=[f"Wg{par}", f"xgT{xpar}"], writes=[f"gq{w}"])
            for j in range(ncc):
                em.op("t", lambda t, j=j, ic=ic, w=w, par=par, xpar=xpar: t.matmul(
                    uq[w][:, 0:C],
                    Wu_b[par][:, j * DI + ic * 128:j * DI + (ic + 1) * 128],
                    xgT[xpar][:, j * C:(j + 1) * C],
                    start=(j == 0), stop=(j == ncc - 1)),
                    reads=[f"Wu{par}", f"xgT{xpar}"], writes=[f"uq{w}"])
            em.op("s", lambda s, w=w: s.activation(sg[w][:], gq[w][:, 0:C],
                                                   AF.Silu),
                  reads=[f"gq{w}"], writes=[f"sg{w}"])
            em.op("v", lambda v, ic=ic, w=w: v.tensor_mul(
                aT[:, ic * C:(ic + 1) * C], sg[w][:], uq[w][:, 0:C]),
                reads=[f"sg{w}", f"uq{w}"], writes=[f"aT_{ic}"])

    def wd_stage(e):
        par = e % NR
        pv = pe_sb[par][:].rearrange("p (s cc) -> p s cc", cc=2)
        for s in range(NSUB):
            for ic in range(icc):
                for half in range(2):
                    em.op("t", lambda t, s=s, ic=ic, half=half, par=par: t.matmul(
                        wd_ps[:, half * 512:(half + 1) * 512],
                        aT[:, ic * C + s * 128:ic * C + s * 128 + 128],
                        Wd_b[par][:, ic * D + half * 512:ic * D + (half + 1) * 512],
                        start=(ic == 0), stop=(ic == icc - 1)),
                        reads=[f"aT_{ic}", f"Wd{par}"], writes=["wdp"])
            w = s % 2
            em.op("v", lambda v, s=s, w=w, par=par: v.tensor_scalar_mul(
                os_sb[w][:], wd_ps[:], pv[:, s, 1:2]),
                reads=["wdp", f"pe{par}"], writes=[f"os{w}"])
            em.dma("s", oslots.ap()[e * C + s * 128:e * C + (s + 1) * 128, :],
                   os_sb[w][:], reads=[f"os{w}"], writes=[f"osl_{e}_{s}"])
            osl_keys.append(f"osl_{e}_{s}")

    load_expert(0)
    gather_stage(0)
    load_expert(1)
    gather_stage(1)
    load_expert(2)
    for e in range(E):
        trans_stage(e)
        gu_stage(e)
        wd_stage(e)
        if e + 3 < E:
            load_expert(e + 3)
        if e + 2 < E:
            gather_stage(e + 2)

    ya_sb = [gc.enter_context(nc.sbuf_tensor(f"ya{i}", [128, D], F32))
             for i in range(2)]
    for k in range(NOC):
        em.dma("q", ya_sb[k % 2][:], y_part_t[k, :, :],
               reads=[f"ypart_{k}"], writes=[f"ya{k % 2}"])
        for j in range(TOP_K):
            em.op("v", lambda v, j=j, k=k: v.tensor_copy(
                idx4[j][:], slots4[:, 4 * k + j:4 * k + j + 1]),
                reads=["slots4"], writes=[f"idx4_{j}"])
            em.idma(og_sb[j % 2][:], None, oslots.ap(),
                    IndirectOffsetOnAxis(idx4[j][:], axis=0),
                    reads=[f"idx4_{j}"] + osl_keys, writes=[f"og{j % 2}"])
            em.op("v", lambda v, j=j, k=k: v.tensor_add(
                ya_sb[k % 2][:], ya_sb[k % 2][:], og_sb[j % 2][:]),
                reads=[f"og{j % 2}"], writes=[f"ya{k % 2}"])
        em.dma("q", y_t[:, k:k + 1, :],
               ya_sb[k % 2][:].rearrange("p (k ch) -> p k ch", ch=D),
               reads=[f"ya{k % 2}"], writes=[f"yhbm_{k}"])
    em.final_wait()

    return nc



def make_inputs(c: CFG, core, xf, seq, wqkv, wgate, wout, gw, Wgg, Wuu, Wdd):
    n_own = c.n_own
    start = core * n_own
    x_own = np.ascontiguousarray(xf[start:start + n_own])
    b = start // seq
    q0 = start - b * seq
    npre = max(c.n_pre, 128)
    x_pre = np.zeros((npre, c.d), np.float32)
    x_pre[:q0] = xf[b * seq:b * seq + q0]
    pm = np.zeros((128, c.npc + c.noc), ml_dtypes.bfloat16)
    pm[:, c.npc:] = 1
    pm[:, :q0 // 128] = 1
    return {
        "x_own": x_own, "x_pre": x_pre,
        "wqkv": wqkv, "wgate": wgate, "wout": wout, "gw": gw,
        "Wg": Wgg, "Wu": Wuu, "Wd": Wdd,
        "pmask": pm,
        "ident": np.eye(128, dtype=ml_dtypes.bfloat16),
        "trimask": np.tile(np.triu(np.ones((128, 128), np.float32)), (1, 4)),
        "triuS": np.triu(np.ones((128, 128), ml_dtypes.bfloat16), 1),
        "iota": np.arange(128, dtype=np.float32).reshape(128, 1),
        "ub": np.ascontiguousarray(np.broadcast_to(
            (np.arange(c.E, dtype=np.float32) + 1.0) * c.cap, (128, c.E))),
    }


def kernel(L0_attn_x, L0_attn_w_qkv, L0_attn_w_gate, L0_attn_w_out,
           L0_ffn_gate_w, L0_ffn_Wg, L0_ffn_Wu, L0_ffn_Wd):
    c = CFG()
    B, T, D = L0_attn_x.shape
    xf = np.ascontiguousarray(np.asarray(L0_attn_x, dtype=np.float32)
                              .reshape(B * T, D))
    wqkv = np.ascontiguousarray(np.asarray(L0_attn_w_qkv, dtype=np.float32))
    wgate = np.ascontiguousarray(np.asarray(L0_attn_w_gate, dtype=np.float32))
    wout = np.ascontiguousarray(np.asarray(L0_attn_w_out, dtype=np.float32))
    gw = np.ascontiguousarray(np.asarray(L0_ffn_gate_w, dtype=np.float32))
    Wgg = np.ascontiguousarray(np.asarray(L0_ffn_Wg, dtype=np.float32))
    Wuu = np.ascontiguousarray(np.asarray(L0_ffn_Wu, dtype=np.float32))
    Wdd = np.ascontiguousarray(np.asarray(L0_ffn_Wd, dtype=np.float32))

    nc = build(c)
    in_maps = [make_inputs(c, core, xf, T, wqkv, wgate, wout, gw, Wgg, Wuu, Wdd)
               for core in range(8)]
    res = run_bass_kernel_spmd(nc, in_maps, core_ids=list(range(8)))
    out = np.concatenate([r["y"] for r in res.results], axis=0)
    return out.reshape(B, T, D)
